# version 10
# baseline (speedup 1.0000x reference)
import math
import numpy as np
import ml_dtypes
import concourse.bass as bass
import concourse.mybir as mybir
from concourse.bass_utils import run_bass_kernel_spmd

F32 = mybir.dt.float32
BF16 = mybir.dt.bfloat16
AF = mybir.ActivationFunctionType
ALU = mybir.AluOpType
AX = mybir.AxisListType

E = 2048
N = 4096
HN = 3
HID = 8192
KC = E // 128
NT = N // 512
MC = N // 128
SCALE = 1.0 / math.sqrt(2048.0)
EPS_M = 1e-5
EPS_F = 1e-6

POOLS = {"qS": 8, "qG": 4, "qA": 4}
SEMS = tuple(f"{q}{i}" for q, n in POOLS.items() for i in range(n)) + (
    "tMM", "sEV", "vEV", "gEV")


class W:

    def __init__(self, cur, eng, sems, T):
        self.cur = cur
        self.eng = eng
        self.sems = sems
        self.T = T
        self.c = {k: 0 for k in SEMS}
        self.qrr = {q: 0 for q in POOLS}

    def wait(self, engname, sem, val):
        if val is None or val <= 0:
            return
        if self.cur == engname:
            self.eng.wait_ge(self.sems[sem], val)

    def waitt(self, engname, tok):
        if tok is None:
            return
        self.wait(engname, tok[0], tok[1])

    def qmark(self, q):
        return [(f"{q}{i}", self.c[f"{q}{i}"]) for i in range(POOLS[q])]

    def waitq(self, engname, mark):
        for tok in mark:
            self.waitt(engname, tok)

    def bump(self, inc):
        if inc is None:
            return None
        self.c[inc[0]] += inc[1]
        return self.c[inc[0]]


def op(w, engname, method, *args, inc=None, **kwargs):
    v = w.bump(inc)
    if w.cur == engname:
        ins = getattr(w.eng, method)(*args, **kwargs)
        if inc is not None:
            ins.then_inc(w.sems[inc[0]], inc[1])
    return v


def dma(w, engname, out, in_, q, accum=None):
    i = w.qrr[q]
    w.qrr[q] = (i + 1) % POOLS[q]
    s = f"{q}{i}"
    prev = w.c[s]
    if prev > 0 and w.cur == engname:
        w.eng.wait_ge(w.sems[s], prev)
    w.c[s] += 16
    if w.cur == engname:
        if accum is None:
            ins = w.eng.dma_start(out=out, in_=in_)
        else:
            ins = w.eng.dma_start(out=out, in_=in_, accum_op=accum)
        ins.then_inc(w.sems[s], 16)
    return (s, w.c[s])


def mm(w, out, lhsT, rhs, start, stop, inc=None):
    v = w.bump(inc)
    if w.cur == "tensor":
        ins = w.eng.matmul(out, lhsT, rhs, start=start, stop=stop)
        if inc is not None:
            ins.then_inc(w.sems[inc[0]], inc[1])
    return v


def vcarry(w, inc):
    op(w, "vector", "engine_nop")
    return op(w, "vector", "engine_nop", inc=inc)


def scarry(w, inc):
    op(w, "scalar", "copy", w.T["flushd"], w.T["flushs"])
    return op(w, "scalar", "copy", w.T["flushd2"], w.T["flushs"], inc=inc)


I32 = mybir.dt.int32


def vrsqrt(w, dst, veps, hh, aa, cc):
    yi = dst.bitcast(I32)
    v = op(w, "vector", "tensor_single_scalar", yi, veps.bitcast(I32), 1,
           ALU.arith_shift_right, inc=("vEV", 1))
    w.wait("vector", "vEV", v)
    v = op(w, "vector", "tensor_scalar", yi, yi, 0x5f3759df, -1,
           ALU.subtract, ALU.mult, inc=("vEV", 1))
    hv = op(w, "vector", "tensor_single_scalar", hh, veps, -0.5, ALU.mult,
            inc=("vEV", 1))
    w.wait("vector", "vEV", hv)
    for _ in range(3):
        v = op(w, "vector", "tensor_mul", aa, dst, dst, inc=("vEV", 1))
        w.wait("vector", "vEV", v)
        v = op(w, "vector", "tensor_scalar", cc, aa, hh, 1.5, ALU.mult, ALU.add,
               inc=("vEV", 1))
        w.wait("vector", "vEV", v)
        v = op(w, "vector", "tensor_mul", dst, dst, cc, inc=("vEV", 1))
        w.wait("vector", "vEV", v)
    return v



def emit_all(w):
    st = {}
    ph0(w, st)
    phP(w, st)
    for h in range(HN):
        phA1(w, st, h)
        phA2(w, st, h)
    phFC(w, st)
    phTH(w, st)
    phGATE(w, st)
    for q in POOLS:
        w.waitq("sync", w.qmark(q))


def ph0(w, st):
    T = w.T
    st["xs"] = dma(w, "gpsimd", T["xs3"], T["xTh"].rearrange("(c p) n -> p c n", p=128), "qG")
    st["ident"] = dma(w, "sync", T["ident16"], T["identh"], "qS")
    op(w, "gpsimd", "memset", T["onesf"], 1.0, inc=("gEV", 1))
    st["consts"] = w.c["gEV"]
    op(w, "scalar", "memzero", T["flushs"])
    op(w, "scalar", "memzero", T["flushd"])
    op(w, "scalar", "memzero", T["flushd2"])


def _precast_pieces(T):
    pieces = []

    def add(dst, src, rows_per_piece):
        n = src.shape[0]
        for r0 in range(0, n, rows_per_piece):
            pieces.append((dst[r0:min(n, r0 + rows_per_piece), :],
                           src[r0:min(n, r0 + rows_per_piece), :]))

    add(T["fc16h"], T["fcTh"], 128)
    for et in range(4):
        for p0 in (0, 64):
            pieces.append((T["agg16h"][et, p0:p0 + 64], T["aggth"][et, p0:p0 + 64]))
    for et in range(16):
        for p0 in (0, 32, 64, 96):
            pieces.append((T["d116h"][et, p0:p0 + 32], T["d1th"][et, p0:p0 + 32]))
    for eh in range(8):
        for p0 in range(0, 128, 16):
            pieces.append((T["d216h"][eh, p0:p0 + 16], T["d2th"][eh, p0:p0 + 16]))
    for et in range(4):
        for p0 in (0, 64):
            pieces.append((T["bc16h"][et, p0:p0 + 64], T["bcwth"][et, p0:p0 + 64]))
    add(T["gw16h"], T["gwh"], 128)
    return pieces


def phP(w, st):
    T = w.T
    pieces = _precast_pieces(T)
    pi = [0]
    pA, xs3 = T["pA"], T["xs3"]
    wch, wvs, pstg = T["wch"], T["wvs"], T["pstg"]

    ev_of, stv, tmm_of = {}, {}, {}
    gtile = [0]
    gchunk = [0]

    def pop_piece():
        if pi[0] < len(pieces):
            dst, src = pieces[pi[0]]
            dma(w, "gpsimd", dst, src, "qG")
            pi[0] += 1

    for h in range(HN):
        for kind in ("q", "k", "v"):
            wsrc = T[f"w{kind}T{h}h"]
            if kind in ("q", "k"):
                dst = T[(f"qtt{h}" if kind == "q" else f"kt{h}") + "h"]
                nchunks, tiles_per = KC, NT
            else:
                dst = T[f"v{h}h"]
                nchunks, tiles_per = 4, MC
            for dc in range(nchunks):
                c = gchunk[0]
                if c >= 2:
                    w.wait("gpsimd", "tMM", tmm_of[c - 2])
                if kind in ("q", "k"):
                    ld = dma(w, "gpsimd", wch[:, c % 2],
                             wsrc[:, dc * 128:(dc + 1) * 128]
                             .rearrange("(c p) m -> p c m", p=128), "qG")
                else:
                    ld = dma(w, "gpsimd", wvs[:, c % 2],
                             wsrc[:, dc * 512:(dc + 1) * 512]
                             .rearrange("(c p) m -> p c m", p=128), "qG")
                pop_piece()
                pop_piece()
                w.waitt("tensor", ld)
                if c == 0:
                    w.waitt("tensor", st["xs"])
                for tn in range(tiles_per):
                    ti = gtile[0]
                    pt = pA[:, (ti % 3) * 512:(ti % 3) * 512 + 512]
                    if ti >= 3:
                        w.wait("tensor", "sEV", ev_of[ti - 3])
                    for k in range(KC):
                        if kind in ("q", "k"):
                            l, r = wch[:, c % 2, k, :], xs3[:, k, tn * 512:(tn + 1) * 512]
                        else:
                            l, r = xs3[:, k, tn * 128:(tn + 1) * 128], wvs[:, c % 2, k, :]
                        mm(w, pt, l, r, start=(k == 0), stop=(k == KC - 1),
                           inc=(("tMM", 1) if k == KC - 1 else None))
                    w.wait("scalar", "tMM", w.c["tMM"])
                    if ti >= 2:
                        w.waitt("scalar", stv[ti - 2])
                    ev_of[ti] = op(w, "scalar", "copy", pstg[:, ti % 2, :], pt,
                                   inc=("sEV", 1))
                    w.wait("sync", "sEV", ev_of[ti])
                    if kind == "q":
                        o = dst[tn, :, dc, :]
                    elif kind == "k":
                        o = dst[dc * 128:(dc + 1) * 128, tn * 512:(tn + 1) * 512]
                    else:
                        o = dst[tn * 128:(tn + 1) * 128, dc * 512:(dc + 1) * 512]
                    stv[ti] = dma(w, "sync", o, pstg[:, ti % 2, :], "qS")
                    gtile[0] += 1
                tmm_of[c] = w.c["tMM"]
                gchunk[0] += 1
    while pi[0] < len(pieces):
        pop_piece()
    st["P_mm"] = w.c["tMM"]
    st["P_qS"] = w.qmark("qS")
    st["P_qG"] = w.qmark("qG")
    st["P_sEV"] = w.c["sEV"]


def phA1(w, st, h):
    T = w.T
    pA = T["pA"]
    qts, sst, wstg, ktch = T["qts2"], T["sst"], T["wstg"], T["ktch"]
    zsl, rzsl, negsc = T["zsl"], T["rzsl"], T["negsc2"]
    qth, kth, wth = T[f"qtt{h}h"], T[f"kt{h}h"], T["wth"]

    prev_mm = st["P_mm"] if h == 0 else st[f"A2_mm_{h-1}"]
    prev_sev = st["P_sEV"] if h == 0 else st[f"A2_sEV_{h-1}"]
    prev_qS = st["P_qS"] if h == 0 else st[f"A2_qS_{h-1}"]
    w.waitq("sync", prev_qS)
    w.waitq("gpsimd", prev_qS)
    w.wait("gpsimd", "tMM", prev_mm)
    qv = []
    for nt in range(NT):
        if nt == 0:
            w.wait("sync", "tMM", prev_mm)
        qv.append(dma(w, "sync", qts[:, nt], qth[nt], "qS"))

    ktv, expv, stv, ev, mmv = {}, {}, {}, {}, {}
    for mc in range(MC):
        if mc >= 2:
            w.wait("gpsimd", "tMM", mmv[mc - 2])
        ktv[mc] = dma(w, "gpsimd", ktch[:, mc % 2],
                      kth[:, mc * 128:(mc + 1) * 128]
                      .rearrange("(c p) m -> p c m", p=128), "qG")
        w.waitt("tensor", ktv[mc])
        for nt in range(NT):
            if mc == 0:
                w.waitt("tensor", qv[nt])
                if nt == 0:
                    w.wait("tensor", "sEV", prev_sev)
            ti = mc * NT + nt
            pt = pA[:, (ti % 3) * 512:(ti % 3) * 512 + 512]
            if ti >= 3:
                w.wait("tensor", "sEV", ev[ti - 3])
            for k in range(KC):
                mm(w, pt, ktch[:, mc % 2, k, :], qts[:, nt, k, :],
                   start=(k == 0), stop=(k == KC - 1),
                   inc=(("tMM", 1) if k == KC - 1 else None))
            w.wait("scalar", "tMM", w.c["tMM"])
            ev[ti] = op(w, "scalar", "activation", sst[:, mc % 2, nt * 512:(nt + 1) * 512],
                        pt, AF.Copy, scale=SCALE, inc=("sEV", 1))
        mmv[mc] = w.c["tMM"]
        if mc >= 2:
            w.wait("vector", "sEV", expv[mc - 2])
        w.wait("vector", "sEV", ev[mc * NT + NT - 1])
        op(w, "vector", "reduce_max", negsc[:, mc % 2:mc % 2 + 1], sst[:, mc % 2, :],
           axis=AX.X, negate=True)
        nv = vcarry(w, ("vEV", 1))
        w.wait("scalar", "vEV", nv)
        if mc >= 2:
            w.waitt("scalar", stv[mc - 2])
        op(w, "scalar", "activation", wstg[:, mc % 2, :], sst[:, mc % 2, :],
           AF.Exp, bias=negsc[:, mc % 2:mc % 2 + 1], scale=1.0,
           accum_out=zsl[:, mc:mc + 1])
        expv[mc] = scarry(w, ("sEV", 1))
        w.wait("vector", "sEV", expv[mc])
        op(w, "vector", "reciprocal", rzsl[:, mc:mc + 1], zsl[:, mc:mc + 1],
           inc=("vEV", 1))
        w.wait("sync", "sEV", expv[mc])
        stv[mc] = dma(w, "sync", wth[:, :, mc, :].rearrange("t p n -> p t n"),
                      wstg[:, mc % 2, :], "qS")
    st[f"A1_mm_{h}"] = w.c["tMM"]
    st[f"A1_rz_{h}"] = w.c["vEV"]
    st[f"A1_sEV_{h}"] = w.c["sEV"]
    st[f"A1_qS_{h}"] = w.qmark("qS")


def phA2(w, st, h):
    T = w.T
    pA, pB = T["pA"], T["pB"]
    vs, wts, atstg = T["vs"], T["wts"], T["atstg"]
    rzsl = T["rzsl"]
    vh, wth, ath = T[f"v{h}h"], T["wth"], T["at_allh"]

    if h == 0:
        w.waitq("scalar", st["P_qS"])
    else:
        w.waitq("scalar", st[f"A2_qG_{h-1}"])

    vv = {}
    for pc in range(4):
        if pc == 0:
            w.wait("sync", "tMM", st[f"A1_mm_{h}"])
        vv[pc] = dma(w, "sync", vs[:, pc * 8:(pc + 1) * 8, :],
                     vh[pc * 1024:(pc + 1) * 1024, :]
                     .rearrange("(c p) d -> p c d", p=128), "qS")
    scv = {}
    for mc in range(MC):
        w.waitt("vector", vv[mc // 8])
        if mc == 0:
            w.wait("vector", "vEV", st[f"A1_rz_{h}"])
        scv[mc] = op(w, "vector", "tensor_scalar_mul", vs[:, mc, :], vs[:, mc, :],
                     rzsl[:, mc:mc + 1], inc=("vEV", 1))

    order = [7] + list(range(7))
    wtv, grp_ev, stv = {}, {}, {}
    gidx = 0
    w.waitq("sync", st[f"A1_qS_{h}"])
    for i, nt in enumerate(order):
        if i >= 2:
            w.wait("sync", "tMM", st[f"A2t_{h}_{i-2}"])
        wtv[i] = dma(w, "sync", wts[:, i % 2], wth[nt], "qS")
        w.waitt("tensor", wtv[i])
        for grp in range(4):
            pX = pA if grp % 2 == 0 else pB
            if gidx >= 2:
                w.wait("tensor", "sEV", grp_ev[gidx - 2])
            elif gidx == 0:
                w.wait("tensor", "sEV", st[f"A1_sEV_{h}"])
            for mc in range(MC):
                if i == 0 and grp == 0:
                    w.wait("tensor", "vEV", scv[mc])
                elif mc == 0:
                    w.wait("tensor", "vEV", scv[MC - 1])
                for dj in range(4):
                    d = grp * 4 + dj
                    mm(w, pX[:, dj * 512:(dj + 1) * 512],
                       vs[:, mc, d * 128:(d + 1) * 128], wts[:, i % 2, mc, :],
                       start=(mc == 0), stop=(mc == MC - 1),
                       inc=(("tMM", 1) if (mc == MC - 1 and dj == 3) else None))
            w.wait("scalar", "tMM", w.c["tMM"])
            if gidx >= 2:
                w.waitt("scalar", stv[gidx - 2])
            val = None
            for dj in range(4):
                val = op(w, "scalar", "activation",
                         atstg[:, gidx % 2, dj, :], pX[:, dj * 512:(dj + 1) * 512],
                         AF.Copy, scale=(1.0 / 3.0),
                         inc=(("sEV", 1) if dj == 3 else None))
            grp_ev[gidx] = val
            w.wait("gpsimd", "sEV", val)
            if h > 0:
                w.waitt("gpsimd", st[f"A2_st_{h-1}"][gidx])
            stv[gidx] = dma(w, "gpsimd",
                            ath[grp * 512:(grp + 1) * 512, nt * 512:(nt + 1) * 512]
                            .rearrange("(c p) n -> p c n", p=128),
                            atstg[:, gidx % 2],
                            "qG", accum=(None if h == 0 else ALU.add))
            gidx += 1
        st[f"A2t_{h}_{i}"] = w.c["tMM"]
    st[f"A2_mm_{h}"] = w.c["tMM"]
    st[f"A2_st_{h}"] = dict(stv)
    st[f"A2_qG_{h}"] = w.qmark("qG")
    st[f"A2_sEV_{h}"] = w.c["sEV"]
    st[f"A2_qS_{h}"] = w.qmark("qS")


def _ln_plain(w, u, outt, eps, sq):
    T = w.T
    negsum, negmean, ssq, veps, rv, rstd = (T["s1"], T["negmean"], T["ssq"], T["veps"],
                                            T["rv"], T["rstd"])
    w.wait("vector", "vEV", w.c["vEV"])
    w.wait("vector", "sEV", w.c["sEV"])
    op(w, "vector", "reduce_sum", negsum, u, axis=AX.X, negate=True)
    a = vcarry(w, ("vEV", 1))
    w.wait("scalar", "vEV", a)
    op(w, "scalar", "mul", negmean, negsum, 1.0 / 2048.0)
    b0 = scarry(w, ("sEV", 1))
    w.wait("scalar", "sEV", b0)
    op(w, "scalar", "activation", sq, u, AF.Square, bias=negmean, accum_out=ssq)
    c = scarry(w, ("sEV", 1))
    w.wait("vector", "sEV", c)
    tv = op(w, "vector", "tensor_scalar", veps, ssq, 1.0 / 2048.0, eps, ALU.mult,
            ALU.add, inc=("vEV", 1))
    w.wait("vector", "vEV", tv)
    vrsqrt(w, rstd, veps, T["nwh"], T["nwa"], T["nwc"])
    return op(w, "vector", "tensor_scalar", outt, u, negmean, rstd, ALU.add, ALU.mult,
              inc=("vEV", 1))


def phFC(w, st):
    T = w.T
    pA, pB = T["pA"], T["pB"]
    fc16s, atb, xt2 = T["fc16s"], T["atb"], T["xt2"]
    u2b, x1b = T["u2b"], T["x1b"]
    fcbr, x1tstg, x1til = T["fcbr"], T["x1tstg"], T["x1til"]

    w.wait("sync", "tMM", st["A2_mm_2"])
    w.waitq("sync", st["A2_qG_2"])
    w.waitq("sync", st["P_qG"])
    w.waitq("scalar", st["A2_qG_2"])
    w.wait("scalar", "tMM", st["A2_mm_2"])
    w.wait("gpsimd", "tMM", st["A2_mm_2"])
    fc16hv = T["fc16h"].rearrange("(c p) n -> p c n", p=128)
    fcv = [dma(w, "scalar" if j % 2 == 0 else "gpsimd",
               fc16s[:, 4 * j:4 * j + 4, :], fc16hv[:, 4 * j:4 * j + 4, :],
               "qA" if j % 2 == 0 else "qG") for j in range(4)]
    fbv = dma(w, "sync", fcbr, T["fcbrh"], "qS")

    st_x1, atv, xv, fcmm = {}, {}, {}, {}
    fcbank, xcons, lnvd, x1bv, tpv, tpcp, xstv = {}, {}, {}, {}, {}, {}, {}

    def stageA(i):
        if i >= 2:
            w.wait("sync", "tMM", fcmm[i - 2])
        atv[i] = dma(w, "sync", atb[:, i % 2],
                     T["at_allh"][:, i * 128:(i + 1) * 128]
                     .rearrange("(c p) n -> p c n", p=128), "qS")
        if i >= 2:
            w.wait("sync", "vEV", xcons[i - 2])
        xv[i] = dma(w, "sync", xt2[:, i % 2], T["xnh"][i * 128:(i + 1) * 128, :], "qS")
        w.waitt("tensor", atv[i])
        if i == 0:
            w.wait("tensor", "sEV", st["A2_sEV_2"])
        u = u2b[:, i % 2]
        if i >= 1:
            w.wait("tensor", "vEV", fcbank[i - 1])
        for k in range(KC):
            if i == 0 and k % 4 == 0:
                w.waitt("tensor", fcv[k // 4])
            for et in range(4):
                pt = pA[:, et * 512:(et + 1) * 512]
                mm(w, pt, atb[:, i % 2, k, :], fc16s[:, k, et * 512:(et + 1) * 512],
                   start=(k == 0), stop=(k == KC - 1),
                   inc=(("tMM", 1) if k == KC - 1 else None))
        fcmm[i] = w.c["tMM"]
        w.wait("vector", "tMM", fcmm[i])
        w.waitt("vector", xv[i])
        if i == 0:
            w.waitt("vector", fbv)
        if i >= 2:
            w.wait("vector", "vEV", lnvd[i - 2])
        fcbank[i] = op(w, "vector", "tensor_add", u, pA[:, 0:2048], fcbr,
                       inc=("vEV", 1))
        w.wait("vector", "vEV", fcbank[i])
        xcons[i] = op(w, "vector", "tensor_add", u, u, xt2[:, i % 2],
                      inc=("vEV", 1))

    def stageB(i):
        u = u2b[:, i % 2]
        if i >= 1:
            w.waitt("vector", st_x1[i - 1])
        lnvd[i] = _ln_plain(w, u, x1til, EPS_M, T["sqt"])
        w.wait("sync", "vEV", lnvd[i])
        st_x1[i] = dma(w, "sync", T["x1h"][i * 128:(i + 1) * 128, :], x1til, "qS")
        if i >= 1:
            w.wait("vector", "tMM", tpv[i - 1])
        w.wait("vector", "vEV", w.c["vEV"])
        x1bv[i] = op(w, "vector", "tensor_copy", x1b, x1til, inc=("vEV", 1))
        w.wait("tensor", "vEV", x1bv[i])
        if i == 0:
            w.waitt("tensor", st["ident"])
        if i >= 2:
            w.wait("tensor", "vEV", tpcp[i - 2])
        pb16 = pB[:, (i % 2) * 1024:(i % 2) * 1024 + 1024].bitcast(BF16)
        for c in range(KC):
            op(w, "tensor", "transpose", pb16[:, c * 128:(c + 1) * 128],
               x1b[:, c * 128:(c + 1) * 128], T["ident16"],
               inc=(("tMM", 1) if c == KC - 1 else None))
        tpv[i] = w.c["tMM"]
        w.wait("vector", "tMM", tpv[i])
        if i >= 2:
            w.waitt("vector", xstv[i - 2])
        tpcp[i] = op(w, "vector", "tensor_copy", x1tstg[:, i % 2], pb16, inc=("vEV", 1))
        w.wait("scalar", "vEV", tpcp[i])
        xstv[i] = dma(w, "scalar", T["x1Th"][i], x1tstg[:, i % 2], "qA")

    for i in range(MC):
        stageA(i)
        if i >= 1:
            stageB(i - 1)
    stageB(MC - 1)
    st["FC_mm"] = w.c["tMM"]
    st["FC_x1"] = w.qmark("qS")
    st["FC_x1T"] = w.qmark("qA")
    st["FC_vEV"] = w.c["vEV"]


def _row_ln_affine(w, src, dst, gam, bet, eps, tmp):
    T = w.T
    negsum, negmean, ssq, veps, rv, rstd = (T["ts1"], T["tnegmean"], T["tssq"],
                                            T["tveps"], T["trv"], T["trstd"])
    w.wait("vector", "vEV", w.c["vEV"])
    w.wait("vector", "sEV", w.c["sEV"])
    op(w, "vector", "reduce_sum", negsum, src, axis=AX.X, negate=True)
    a = vcarry(w, ("vEV", 1))
    w.wait("scalar", "vEV", a)
    op(w, "scalar", "mul", negmean, negsum, 1.0 / 2048.0)
    b0 = scarry(w, ("sEV", 1))
    w.wait("scalar", "sEV", b0)
    op(w, "scalar", "activation", tmp, src, AF.Square, bias=negmean, accum_out=ssq)
    c = scarry(w, ("sEV", 1))
    w.wait("vector", "sEV", c)
    tv = op(w, "vector", "tensor_scalar", veps, ssq, 1.0 / 2048.0, eps, ALU.mult,
            ALU.add, inc=("vEV", 1))
    w.wait("vector", "vEV", tv)
    vrsqrt(w, rstd, veps, T["tnwh"], T["tnwa"], T["tnwc"])
    t1 = op(w, "vector", "tensor_scalar", dst, src, negmean, rstd, ALU.add, ALU.mult,
            inc=("vEV", 1))
    w.wait("vector", "vEV", t1)
    t2 = op(w, "vector", "tensor_mul", dst, dst, gam, inc=("vEV", 1))
    w.wait("vector", "vEV", t2)
    return op(w, "vector", "tensor_add", dst, dst, bet, inc=("vEV", 1))


def _row_to_col(w, row16, ncols, pcol, dst16):
    T = w.T
    w.wait("tensor", "sEV", w.c["sEV"])
    for c in range(ncols):
        mm(w, pcol[:, c:c + 1], row16[0:1, c * 128:(c + 1) * 128],
           T["basis16"][0:1, 0:1], start=True, stop=True,
           inc=(("tMM", 1) if c == ncols - 1 else None))
    w.wait("scalar", "tMM", w.c["tMM"])
    return op(w, "scalar", "copy", dst16, pcol[:, 0:ncols], inc=("sEV", 1))


def phTH(w, st):
    T = w.T
    pA, pB = T["pA"], T["pB"]
    d1sl, d2sl = T["d1sl"], T["d2sl"]
    trow, tmpr, tvec, tn = T["trow"], T["tmpr"], T["tvec"], T["tn"]
    ing, inb, png, pnb = T["ing_s"], T["inb_s"], T["png_s"], T["pnb_s"]
    tn16, h1row16 = T["tn16"], T["h1row16"]
    tT16, h1T16 = T["tT16"], T["h1T16"]
    bcr = T["bcr"]

    w.waitq("sync", st["FC_x1"])
    w.waitq("sync", st["FC_x1T"])
    w.waitq("sync", st["P_qG"])
    w.wait("sync", "tMM", st["FC_mm"])
    w.wait("scalar", "tMM", st["FC_mm"])
    dma(w, "sync", trow, T["x1h"][N - 1:N, :], "qS")
    dma(w, "sync", ing, T["ingh"], "qS")
    dma(w, "sync", inb, T["inbh"], "qS")
    dma(w, "sync", png, T["pngh"], "qS")
    lv = dma(w, "sync", pnb, T["pnbh"], "qS")

    w.waitq("vector", w.qmark("qS"))
    w.wait("vector", "tMM", st["FC_mm"])
    t0v = _row_ln_affine(w, trow, tmpr, ing, inb, EPS_F, T["tmprow2"])
    w.wait("vector", "vEV", w.c["vEV"])
    cv = op(w, "vector", "tensor_copy", tn16, tmpr, inc=("vEV", 1))
    w.wait("tensor", "vEV", cv)
    w.wait("tensor", "gEV", st["consts"])
    rcv = _row_to_col(w, tn16, KC, pB[:, 0:16], tT16)
    w.wait("tensor", "sEV", rcv)

    def slab17(buf, srcmat, et, eng="sync", q="qS"):
        t1 = dma(w, eng, buf[:, :, :].rearrange("p c n -> p (c n)"),
                 srcmat[et], q)
        return [t1]

    def mm17(pt, slab, colT, nk):
        for k in range(nk):
            mm(w, pt, colT[:, k:k + 1], slab[:, k, :], start=(k == 0), stop=False)
        mm(w, pt, T["basis16"][:, 0:1], slab[:, nk, :], start=False, stop=True,
           inc=("tMM", 1))
        return w.c["tMM"]

    d1cnt = [0]
    d1mm = {}

    def d1buf():
        c = d1cnt[0]
        d1cnt[0] += 1
        return c, c % 3

    aggv, pev = {}, {}
    for et in range(4):
        c, b = d1buf()
        eng, q = (("sync", "qS") if et % 2 == 0 else ("scalar", "qA"))
        if c >= 3:
            w.wait(eng, "tMM", d1mm[c - 3])
        v = slab17(d1sl[:, b], T["agg16h"], et, eng, q)
        w.waitq("tensor", v)
        if et >= 3:
            w.wait("tensor", "vEV", pev[et - 3])
        pt = pA[0:1, (et % 3) * 512:(et % 3) * 512 + 512]
        aggv[et] = mm17(pt, d1sl[:, b], tT16, KC)
        d1mm[c] = aggv[et]
        w.wait("vector", "tMM", aggv[et])
        pev[et] = op(w, "vector", "tensor_copy", tvec[0:1, et * 512:(et + 1) * 512], pt,
                     inc=("vEV", 1))

    h1g = {}
    d2mm = {}
    for s in range(4):
        _row_ln_affine(w, tvec, tn, png, pnb, EPS_F, T["tmprow2"])
        w.wait("vector", "vEV", w.c["vEV"])
        cv = op(w, "vector", "tensor_copy", tn16, tn, inc=("vEV", 1))
        w.wait("tensor", "vEV", cv)
        rcv = _row_to_col(w, tn16, KC, pB[:, 0:16], tT16)
        w.wait("tensor", "sEV", rcv)
        for ht in range(16):
            c, b = d1buf()
            eng, q = (("sync", "qS") if ht % 2 == 0 else ("scalar", "qA"))
            if c >= 3:
                w.wait(eng, "tMM", d1mm[c - 3])
            v = slab17(d1sl[:, b], T["d116h"], ht, eng, q)
            w.waitq("tensor", v)
            pt = pA[0:1, (ht % 3) * 512:(ht % 3) * 512 + 512]
            if ht >= 3:
                w.wait("tensor", "sEV", h1g[(s, ht - 3, "g")])
            h1g[(s, ht, "mm")] = mm17(pt, d1sl[:, b], tT16, KC)
            d1mm[c] = h1g[(s, ht, "mm")]
            w.wait("scalar", "tMM", h1g[(s, ht, "mm")])
            h1g[(s, ht, "g")] = op(w, "scalar", "activation",
                                   h1row16[0:1, ht * 512:(ht + 1) * 512], pt, AF.Gelu,
                                   inc=("sEV", 1))
        w.wait("tensor", "sEV", h1g[(s, 15, "g")])
        h1c = _row_to_col(w, h1row16, 64, pB[:, 16:80], h1T16)
        w.wait("tensor", "sEV", h1c)
        for eh in range(8):
            eng, q = (("sync", "qS") if eh % 2 == 0 else ("scalar", "qA"))
            if eh >= 2:
                w.wait(eng, "tMM", d2mm[(s, eh - 2)])
            elif s > 0:
                w.wait(eng, "tMM", d2mm[(s - 1, 6 + eh)])
            elif eh == 0:
                w.wait(eng, "vEV", t0v)
            v1 = dma(w, eng, d2sl[:, eh % 2].rearrange("p c n -> p (c n)"),
                     T["d216h"][eh], q)
            w.waitt("tensor", v1)
            pt = pA[0:1, (eh % 3) * 512:(eh % 3) * 512 + 256]
            if eh >= 3:
                w.wait("tensor", "vEV", d2mm[(s, eh - 3, "v")])
            for k in range(64):
                mm(w, pt, h1T16[:, k:k + 1], d2sl[:, eh % 2, k, :],
                   start=(k == 0), stop=False)
            mm(w, pt, T["basis16"][:, 0:1], d2sl[:, eh % 2, 64, :],
               start=False, stop=True, inc=("tMM", 1))
            d2mm[(s, eh)] = w.c["tMM"]
            w.wait("vector", "tMM", d2mm[(s, eh)])
            d2mm[(s, eh, "v")] = op(w, "vector", "tensor_add",
                                    tvec[0:1, eh * 256:(eh + 1) * 256],
                                    tvec[0:1, eh * 256:(eh + 1) * 256], pt,
                                    inc=("vEV", 1))

    w.wait("vector", "vEV", w.c["vEV"])
    cv = op(w, "vector", "tensor_copy", tn16, tvec, inc=("vEV", 1))
    w.wait("tensor", "vEV", cv)
    rcv = _row_to_col(w, tn16, KC, pB[:, 0:16], tT16)
    w.wait("tensor", "sEV", rcv)
    bcv = {}
    for et in range(4):
        c, b = d1buf()
        eng, q = (("sync", "qS") if et % 2 == 0 else ("scalar", "qA"))
        if c >= 3:
            w.wait(eng, "tMM", d1mm[c - 3])
        v = slab17(d1sl[:, b], T["bc16h"], et, eng, q)
        w.waitq("tensor", v)
        pt = pA[0:1, (et % 3) * 512:(et % 3) * 512 + 512]
        if et >= 3:
            w.wait("tensor", "vEV", bcv[(et - 3, "v")])
        bcv[et] = mm17(pt, d1sl[:, b], tT16, KC)
        d1mm[c] = bcv[et]
        w.wait("vector", "tMM", bcv[et])
        bcv[(et, "v")] = op(w, "vector", "tensor_copy",
                            bcr[0:1, et * 512:(et + 1) * 512], pt, inc=("vEV", 1))
    w.wait("tensor", "vEV", bcv[(3, "v")])
    for c in range(4):
        mm(w, pA[:, c * 512:(c + 1) * 512], T["onesf"][0:1, :],
           bcr[0:1, c * 512:(c + 1) * 512], start=True, stop=True,
           inc=(("tMM", 1) if c == 3 else None))
    w.wait("vector", "tMM", w.c["tMM"])
    st["bct_v"] = op(w, "vector", "tensor_copy", T["bct"], pA[:, 0:2048], inc=("vEV", 1))
    st["TH_mm"] = w.c["tMM"]


def phGATE(w, st):
    T = w.T
    pA = T["pA"]
    gw16s, x1tch, x1t2, gt2 = T["gw16s"], T["x1tch"], T["x1t2"], T["gt2"]
    ot, ft, u2t = T["ot"], T["ft"], T["u2t"]
    outstg = T["outstg"]
    gbr, ogr, obr, bct = T["gbr"], T["ogr"], T["obr"], T["bct"]

    w.wait("sync", "tMM", st["TH_mm"])
    w.wait("sync", "vEV", st["bct_v"])
    w.waitq("sync", st["FC_x1"])
    w.waitq("sync", st["FC_x1T"])
    w.wait("scalar", "tMM", st["TH_mm"])
    w.wait("gpsimd", "tMM", st["TH_mm"])
    gw16hv = T["gw16h"].rearrange("(c p) n -> p c n", p=128)
    gv = [dma(w, "scalar" if j % 2 == 0 else "gpsimd",
              gw16s[:, 4 * j:4 * j + 4, :], gw16hv[:, 4 * j:4 * j + 4, :],
              "qA" if j % 2 == 0 else "qG") for j in range(4)]
    st["gbr_v"] = dma(w, "sync", T["gbr"], T["gbrh"], "qS")
    st["ogr_v"] = dma(w, "sync", T["ogr"], T["ogrh"], "qS")
    st["obr_v"] = dma(w, "sync", T["obr"], T["obrh"], "qS")

    xtv, x1v, mmv, stv, u2v, sgv, gaddv, gconsv = {}, {}, {}, {}, {}, {}, {}, {}
    pB = T["pB"]

    def stageA(i):
        gt = gt2[:, i % 2]
        pX = pA if i % 2 == 0 else pB
        if i >= 2:
            w.wait("sync", "tMM", mmv[i - 2])
        xtv[i] = dma(w, "sync", x1tch[:, i % 2].rearrange("p c n -> p (c n)"),
                     T["x1Th"][i], "qS")
        if i >= 2:
            w.wait("sync", "vEV", u2v[i - 2])
        x1v[i] = dma(w, "sync", x1t2[:, i % 2], T["x1h"][i * 128:(i + 1) * 128, :], "qS")
        w.waitt("tensor", xtv[i])
        if i == 0:
            w.wait("tensor", "vEV", st["bct_v"])
        if i >= 2:
            w.wait("tensor", "vEV", gaddv[i - 2])
        for k in range(KC):
            if i == 0 and k % 4 == 0:
                w.waitt("tensor", gv[k // 4])
            for et in range(4):
                pt = pX[:, et * 512:(et + 1) * 512]
                mm(w, pt, x1tch[:, i % 2, k, :], gw16s[:, k, et * 512:(et + 1) * 512],
                   start=(k == 0), stop=(k == KC - 1),
                   inc=(("tMM", 1) if k == KC - 1 else None))
        mmv[i] = w.c["tMM"]
        w.wait("vector", "tMM", mmv[i])
        if i == 0:
            w.waitt("vector", st["gbr_v"])
        if i >= 2:
            w.wait("vector", "vEV", gconsv[i - 2])
        gaddv[i] = op(w, "vector", "tensor_add", gt, pX[:, 0:2048], gbr, inc=("vEV", 1))
        w.wait("scalar", "vEV", gaddv[i])
        op(w, "scalar", "activation", gt, gt, AF.Sigmoid, inc=("sEV", 1))
        sgv[i] = w.c["sEV"]

    def stageB(i):
        gt = gt2[:, i % 2]
        w.wait("vector", "sEV", sgv[i])
        if i == 0:
            w.waitt("vector", st["ogr_v"])
            w.waitt("vector", st["obr_v"])
        gconsv[i] = op(w, "vector", "tensor_mul", ot, gt, bct, inc=("vEV", 1))
        w.wait("vector", "vEV", gconsv[i])
        w.waitt("vector", x1v[i])
        op(w, "vector", "tensor_add", ot, ot, x1t2[:, i % 2], inc=("vEV", 1))
        lff = _ln_plain(w, ot, ft, EPS_F, T["sqg"])
        w.wait("vector", "vEV", lff)
        t1 = op(w, "vector", "tensor_mul", ft, ft, ogr, inc=("vEV", 1))
        w.wait("vector", "vEV", t1)
        t2 = op(w, "vector", "tensor_add", ft, ft, obr, inc=("vEV", 1))
        w.wait("vector", "vEV", t2)
        u2v[i] = op(w, "vector", "tensor_add", u2t, ft, x1t2[:, i % 2], inc=("vEV", 1))
        if i >= 2:
            w.waitt("vector", stv[i - 2])
        lnv = _ln_plain(w, u2t, outstg[:, i % 2], EPS_M, T["sqg"])
        w.wait("sync", "vEV", lnv)
        stv[i] = dma(w, "sync", T["outh"][i * 128:(i + 1) * 128, :], outstg[:, i % 2], "qS")

    for i in range(MC):
        stageA(i)
        if i >= 1:
            stageB(i - 1)
    stageB(MC - 1)



def build_nc(dbg=False):
    nc = bass.Bass("TRN2", target_bir_lowering=False)
    T = {}

    def din(name, shape, dt=F32):
        T[name + "h"] = nc.dram_tensor(name, shape, dt, kind="ExternalInput").ap()

    def dscr(name, shape, dt):
        kind = "ExternalOutput" if dbg else "Internal"
        T[name + "h"] = nc.dram_tensor(name, shape, dt, kind=kind).ap()

    din("xT", [E, N]); din("xn", [N, E])
    for h in range(HN):
        din(f"wqT{h}", [E, E]); din(f"wkT{h}", [E, E]); din(f"wvT{h}", [E, E])
    din("fcT", [E, E]); din("gw", [E, E])
    din("aggt", [4, 128, 17, 512]); din("d1t", [16, 128, 17, 512])
    din("d2t", [8, 128, 65, 256]); din("bcwt", [4, 128, 17, 512])
    din("fcbr", [128, E]); din("gbr", [128, E]); din("ogr", [128, E]); din("obr", [128, E])
    din("ing", [1, E]); din("inb", [1, E]); din("png", [1, E]); din("pnb", [1, E])
    din("aggb", [1, E]); din("d2b", [1, E]); din("bcb", [1, E]); din("d1b", [1, HID])
    din("ident", [128, 128], BF16)
    T["outh"] = nc.dram_tensor("out", [N, E], F32, kind="ExternalOutput").ap()

    for h in range(HN):
        dscr(f"qt{h}", [E, N], BF16); dscr(f"kt{h}", [E, N], BF16)
        dscr(f"v{h}", [N, E], BF16)
    dscr("at_all", [E, N], BF16)
    dscr("wt", [NT, 128, MC, 512], BF16)
    dscr("x1", [N, E], F32); dscr("x1T", [MC, 128, E], BF16)
    for h in range(HN):
        dscr(f"qtt{h}", [NT, 128, KC, 512], BF16)
    dscr("fc16", [E, E], BF16); dscr("agg16", [4, 128, 17, 512], BF16)
    dscr("d116", [16, 128, 17, 512], BF16); dscr("d216", [8, 128, 65, 256], BF16)
    dscr("bc16", [4, 128, 17, 512], BF16); dscr("gw16", [E, E], BF16)

    A = nc.alloc_sbuf_tensor("arenaA", [128, 131072], mybir.dt.uint8).ap()
    B = nc.alloc_sbuf_tensor("arenaB", [128, 65536], mybir.dt.uint8).ap()
    R = nc.alloc_sbuf_tensor("rowsA", [128, 8192], mybir.dt.uint8).ap()
    S = nc.alloc_sbuf_tensor("statsA", [128, 2048], mybir.dt.uint8).ap()
    A16, A32 = A.bitcast(BF16), A.bitcast(F32)
    B16, B32 = B.bitcast(BF16), B.bitcast(F32)
    R16, R32 = R.bitcast(BF16), R.bitcast(F32)
    S16, S32 = S.bitcast(BF16), S.bitcast(F32)

    T["xs3"] = A16[:, :].rearrange("p (c n) -> p c n", c=KC)
    T["wch"] = B16[:, 0:2 * KC * 128].rearrange("p (b c m) -> p b c m", b=2, c=KC)
    T["wvs"] = B16[:, 4096:4096 + 2 * KC * 512].rearrange("p (b c m) -> p b c m", b=2, c=KC)
    T["pstg"] = R16[:, 0:1024].rearrange("p (b n) -> p b n", b=2)
    T["qts2"] = A16[:, :].rearrange("p (t c n) -> p t c n", t=NT, c=KC)
    T["sst"] = B32[:, 0:8192].rearrange("p (b n) -> p b n", b=2)
    T["wstg"] = B16[:, 16384:24576].rearrange("p (b n) -> p b n", b=2)
    T["ktch"] = B16[:, 24576:28672].rearrange("p (b c m) -> p b c m", b=2, c=KC)
    T["vs"] = A16[:, :].rearrange("p (c d) -> p c d", c=MC)
    T["wts"] = B16[:, :].rearrange("p (b c n) -> p b c n", b=2, c=MC)
    T["atstg"] = R16[:, 0:4096].rearrange("p (b c n) -> p b c n", b=2, c=4)
    T["fc16s"] = A16[:, 0:KC * 2048].rearrange("p (c n) -> p c n", c=KC)
    T["atb"] = A16[:, 32768:32768 + 2 * 2048].rearrange("p (b c n) -> p b c n",
                                                        b=2, c=KC)
    T["xt2"] = A32[:, 23552:23552 + 4096].rearrange("p (b n) -> p b n", b=2)
    T["u2b"] = B32[:, 6144:10240].rearrange("p (b n) -> p b n", b=2)
    T["sqt"] = B32[:, 10240:12288]
    T["x1b"] = A16[:, 63488:63488 + 2048]
    T["fcbr"] = B32[:, 0:2048]
    T["x1til"] = B32[:, 2048:4096]
    T["x1tstg"] = B16[:, 8192:12288].rearrange("p (b n) -> p b n", b=2)
    T["d1sl"] = A16[:, 0:3 * 17 * 512].rearrange("p (b c n) -> p b c n", b=3, c=17)
    T["d2sl"] = A16[:, 26112:26112 + 2 * 65 * 256].rearrange("p (b c n) -> p b c n",
                                                             b=2, c=65)
    T["ing_s"] = A32[0:1, 13056:15104]
    T["inb_s"] = A32[0:1, 15104:17152]
    T["tmpr"] = A32[0:1, 29696:31744]
    T["tmprow2"] = T["tmpr"]
    T["trow"] = R32[0:1, 0:2048]
    T["png_s"] = B32[0:1, 2048:4096]
    T["pnb_s"] = B32[0:1, 4096:6144]
    T["tvec"] = B32[0:1, 6144:8192]
    T["tn"] = B32[0:1, 8192:10240]
    T["bcr"] = B32[0:1, 8192:10240]
    T["h1row16"] = B16[0:1, 20480:28672]
    T["tn16"] = B16[0:1, 28672:30720]
    T["gw16s"] = A16[:, 0:KC * 2048].rearrange("p (c n) -> p c n", c=KC)
    T["x1tch"] = A16[:, 32768:36864].rearrange("p (b c n) -> p b c n", b=2, c=KC)
    T["x1t2"] = A32[:, 18432:22528].rearrange("p (b n) -> p b n", b=2)
    T["gt2"] = B32[:, 12288:16384].rearrange("p (b n) -> p b n", b=2)
    T["ot"] = A32[:, 24576:26624]
    T["ft"] = A32[:, 26624:28672]
    T["sqg"] = A32[:, 28672:30720]
    T["u2t"] = A32[:, 30720:32768]
    T["bct"] = B32[0:128, 0:2048]
    T["gbr"] = B32[:, 2048:4096]
    T["ogr"] = B32[:, 4096:6144]
    T["obr"] = B32[:, 6144:8192]
    T["outstg"] = B32[:, 8192:12288].rearrange("p (b n) -> p b n", b=2)
    T["s1"] = S32[:, 0:1]; T["negmean"] = S32[:, 1:2]; T["ssq"] = S32[:, 3:4]
    T["veps"] = S32[:, 4:5]; T["rv"] = S32[:, 5:6]; T["rstd"] = S32[:, 6:7]
    T["cmax"] = S32[:, 7:8]
    T["ts1"] = S32[0:1, 8:9]; T["tnegmean"] = S32[0:1, 9:10]; T["tssq"] = S32[0:1, 10:11]
    T["tveps"] = S32[0:1, 11:12]; T["trv"] = S32[0:1, 12:13]; T["trstd"] = S32[0:1, 13:14]
    T["zsl"] = S32[:, 128:160]; T["rzsl"] = S32[:, 160:192]
    T["cm8"] = S32[:, 192:200]; T["negsc2"] = S32[:, 200:202]
    T["tT16"] = S16[:, 416:432]; T["h1T16"] = S16[:, 432:496]
    T["ident16"] = S16[:, 496:624]
    T["basis16"] = T["ident16"][:, 0:1]
    T["onesf"] = S32[0:1, 336:464]

    T["negsl"] = R32[:, 1024:1040]
    T["flushd"] = S32[:, 14:15]
    T["flushs"] = S32[:, 15:16]
    T["flushd2"] = S32[:, 202:203]
    T["nwh"] = S32[:, 16:17]; T["nwa"] = S32[:, 17:18]; T["nwc"] = S32[:, 18:19]
    T["tnwh"] = S32[0:1, 20:21]; T["tnwa"] = S32[0:1, 21:22]; T["tnwc"] = S32[0:1, 22:23]
    T["pA"] = nc.alloc_psum_tensor("pA", [128, 2048], F32).ap()
    T["pB"] = nc.alloc_psum_tensor("pB", [128, 2048], F32).ap()

    from contextlib import ExitStack
    es = ExitStack()
    sems = {s: es.enter_context(nc.semaphore(s)) for s in SEMS}

    with nc.Block() as blk:
        def mk(curname):
            def f(eng):
                emit_all(W(curname, eng, sems, T))
            return f
        blk.tensor(mk("tensor"))
        blk.vector(mk("vector"))
        blk.scalar(mk("scalar"))
        blk.sync(mk("sync"))
        blk.gpsimd(mk("gpsimd"))
    return nc



_NC_CACHE = {}


def _get_nc(dbg=False):
    if dbg not in _NC_CACHE:
        _NC_CACHE[dbg] = build_nc(dbg)
    return _NC_CACHE[dbg]


def make_in_maps(L0_attn_x, L0_attn_Wq, L0_attn_Wk, L0_attn_Wv, L0_attn_fc_w,
                 L0_attn_fc_b, L0_ffn_in_gamma, L0_ffn_in_beta, L0_ffn_agg_w,
                 L0_ffn_agg_b, L0_ffn_pn_gamma, L0_ffn_pn_beta, L0_ffn_d1_w,
                 L0_ffn_d1_b, L0_ffn_d2_w, L0_ffn_d2_b, L0_ffn_bc_w, L0_ffn_bc_b,
                 L0_ffn_gate_w, L0_ffn_gate_b, L0_ffn_out_gamma, L0_ffn_out_beta):
    f = np.float32
    x = np.asarray(L0_attn_x, f)
    shared = {}
    for h in range(HN):
        shared[f"wqT{h}"] = np.ascontiguousarray(np.asarray(L0_attn_Wq, f)[h].T)
        shared[f"wkT{h}"] = np.ascontiguousarray(np.asarray(L0_attn_Wk, f)[h].T)
        shared[f"wvT{h}"] = np.ascontiguousarray(np.asarray(L0_attn_Wv, f)[h].T)
    shared["fcT"] = np.ascontiguousarray(np.asarray(L0_attn_fc_w, f).T)
    shared["gw"] = np.ascontiguousarray(np.asarray(L0_ffn_gate_w, f))

    def tile_aug(wmat, bias, tcount, csz, nsz):
        out = np.zeros((tcount, 128, csz + 1, nsz), f)
        out[:, :, 0:csz, :] = wmat.reshape(csz, 128, tcount, nsz).transpose(2, 1, 0, 3)
        out[:, 0, csz, :] = bias.reshape(tcount, nsz)
        return out

    shared["aggt"] = tile_aug(np.asarray(L0_ffn_agg_w, f), np.asarray(L0_ffn_agg_b, f),
                              4, 16, 512)
    shared["d1t"] = tile_aug(np.asarray(L0_ffn_d1_w, f), np.asarray(L0_ffn_d1_b, f),
                             16, 16, 512)
    shared["d2t"] = tile_aug(np.asarray(L0_ffn_d2_w, f), np.asarray(L0_ffn_d2_b, f),
                             8, 64, 256)
    shared["bcwt"] = tile_aug(np.asarray(L0_ffn_bc_w, f), np.asarray(L0_ffn_bc_b, f),
                              4, 16, 512)
    shared["fcbr"] = np.ascontiguousarray(np.broadcast_to(np.asarray(L0_attn_fc_b, f), (128, E)))
    shared["gbr"] = np.ascontiguousarray(np.broadcast_to(np.asarray(L0_ffn_gate_b, f), (128, E)))
    shared["ogr"] = np.ascontiguousarray(np.broadcast_to(np.asarray(L0_ffn_out_gamma, f), (128, E)))
    shared["obr"] = np.ascontiguousarray(np.broadcast_to(np.asarray(L0_ffn_out_beta, f), (128, E)))
    shared["ing"] = np.asarray(L0_ffn_in_gamma, f).reshape(1, E)
    shared["inb"] = np.asarray(L0_ffn_in_beta, f).reshape(1, E)
    shared["png"] = np.asarray(L0_ffn_pn_gamma, f).reshape(1, E)
    shared["pnb"] = np.asarray(L0_ffn_pn_beta, f).reshape(1, E)
    shared["aggb"] = np.asarray(L0_ffn_agg_b, f).reshape(1, E)
    shared["d2b"] = np.asarray(L0_ffn_d2_b, f).reshape(1, E)
    shared["bcb"] = np.asarray(L0_ffn_bc_b, f).reshape(1, E)
    shared["d1b"] = np.asarray(L0_ffn_d1_b, f).reshape(1, HID)
    shared["ident"] = np.eye(128, dtype=ml_dtypes.bfloat16)
    in_maps = []
    for c in range(8):
        m = dict(shared)
        m["xT"] = np.ascontiguousarray(x[c].T)
        m["xn"] = np.ascontiguousarray(x[c])
        in_maps.append(m)
    return in_maps


def kernel(**inputs):
    in_maps = make_in_maps(**inputs)
    nc = _get_nc(dbg=False)
    res = run_bass_kernel_spmd(nc, in_maps, core_ids=list(range(8)))
    return np.stack([r["out"] for r in res.results], axis=0)


# version 11
# speedup vs baseline: 1.0377x; 1.0377x over previous
import math
import numpy as np
import ml_dtypes
import concourse.bass as bass
import concourse.mybir as mybir
from concourse.bass_utils import run_bass_kernel_spmd

F32 = mybir.dt.float32
BF16 = mybir.dt.bfloat16
AF = mybir.ActivationFunctionType
ALU = mybir.AluOpType
AX = mybir.AxisListType

E = 2048
N = 4096
HN = 3
HID = 8192
KC = E // 128
NT = N // 512
MC = N // 128
SCALE = 1.0 / math.sqrt(2048.0)
EPS_M = 1e-5
EPS_F = 1e-6

POOLS = {"qS": 8, "qG": 4, "qA": 4}
SEMS = tuple(f"{q}{i}" for q, n in POOLS.items() for i in range(n)) + (
    "tMM", "sEV", "vEV", "gEV")


class W:

    def __init__(self, cur, eng, sems, T):
        self.cur = cur
        self.eng = eng
        self.sems = sems
        self.T = T
        self.c = {k: 0 for k in SEMS}
        self.qrr = {q: 0 for q in POOLS}

    def wait(self, engname, sem, val):
        if val is None or val <= 0:
            return
        if self.cur == engname:
            self.eng.wait_ge(self.sems[sem], val)

    def waitt(self, engname, tok):
        if tok is None:
            return
        self.wait(engname, tok[0], tok[1])

    def qmark(self, q):
        return [(f"{q}{i}", self.c[f"{q}{i}"]) for i in range(POOLS[q])]

    def waitq(self, engname, mark):
        for tok in mark:
            self.waitt(engname, tok)

    def bump(self, inc):
        if inc is None:
            return None
        self.c[inc[0]] += inc[1]
        return self.c[inc[0]]


def op(w, engname, method, *args, inc=None, **kwargs):
    v = w.bump(inc)
    if w.cur == engname:
        ins = getattr(w.eng, method)(*args, **kwargs)
        if inc is not None:
            ins.then_inc(w.sems[inc[0]], inc[1])
    return v


def dma(w, engname, out, in_, q, accum=None):
    i = w.qrr[q]
    w.qrr[q] = (i + 1) % POOLS[q]
    s = f"{q}{i}"
    prev = w.c[s]
    if prev > 0 and w.cur == engname:
        w.eng.wait_ge(w.sems[s], prev)
    w.c[s] += 16
    if w.cur == engname:
        if accum is None:
            ins = w.eng.dma_start(out=out, in_=in_)
        else:
            ins = w.eng.dma_start(out=out, in_=in_, accum_op=accum)
        ins.then_inc(w.sems[s], 16)
    return (s, w.c[s])


def mm(w, out, lhsT, rhs, start, stop, inc=None):
    v = w.bump(inc)
    if w.cur == "tensor":
        ins = w.eng.matmul(out, lhsT, rhs, start=start, stop=stop)
        if inc is not None:
            ins.then_inc(w.sems[inc[0]], inc[1])
    return v


def vcarry(w, inc):
    op(w, "vector", "engine_nop")
    return op(w, "vector", "engine_nop", inc=inc)


def scarry(w, inc):
    op(w, "scalar", "copy", w.T["flushd"], w.T["flushs"])
    return op(w, "scalar", "copy", w.T["flushd2"], w.T["flushs"], inc=inc)


I32 = mybir.dt.int32


def vrsqrt(w, dst, veps, hh, aa, cc):
    yi = dst.bitcast(I32)
    v = op(w, "vector", "tensor_single_scalar", yi, veps.bitcast(I32), 1,
           ALU.arith_shift_right, inc=("vEV", 1))
    w.wait("vector", "vEV", v)
    v = op(w, "vector", "tensor_scalar", yi, yi, 0x5f3759df, -1,
           ALU.subtract, ALU.mult, inc=("vEV", 1))
    hv = op(w, "vector", "tensor_single_scalar", hh, veps, -0.5, ALU.mult,
            inc=("vEV", 1))
    w.wait("vector", "vEV", hv)
    for _ in range(3):
        v = op(w, "vector", "tensor_mul", aa, dst, dst, inc=("vEV", 1))
        w.wait("vector", "vEV", v)
        v = op(w, "vector", "tensor_scalar", cc, aa, hh, 1.5, ALU.mult, ALU.add,
               inc=("vEV", 1))
        w.wait("vector", "vEV", v)
        v = op(w, "vector", "tensor_mul", dst, dst, cc, inc=("vEV", 1))
        w.wait("vector", "vEV", v)
    return v



def emit_all(w):
    st = {}
    ph0(w, st)
    phP(w, st)
    for h in range(HN):
        phA1(w, st, h)
        phA2(w, st, h)
    phFC(w, st)
    phTH(w, st)
    phGATE(w, st)
    for q in POOLS:
        w.waitq("sync", w.qmark(q))


def ph0(w, st):
    T = w.T
    st["xs"] = dma(w, "gpsimd", T["xs3"], T["xTh"].rearrange("(c p) n -> p c n", p=128), "qG")
    st["ident"] = dma(w, "sync", T["ident16"], T["identh"], "qS")
    op(w, "gpsimd", "memset", T["onesf"], 1.0, inc=("gEV", 1))
    st["consts"] = w.c["gEV"]
    op(w, "scalar", "memzero", T["flushs"])
    op(w, "scalar", "memzero", T["flushd"])
    op(w, "scalar", "memzero", T["flushd2"])


def _precast_pieces(T):
    pieces = []

    def add(dst, src, rows_per_piece):
        n = src.shape[0]
        for r0 in range(0, n, rows_per_piece):
            pieces.append((dst[r0:min(n, r0 + rows_per_piece), :],
                           src[r0:min(n, r0 + rows_per_piece), :]))

    add(T["fc16h"], T["fcTh"], 128)
    for et in range(4):
        for p0 in (0, 64):
            pieces.append((T["agg16h"][et, p0:p0 + 64], T["aggth"][et, p0:p0 + 64]))
    for et in range(16):
        for p0 in (0, 32, 64, 96):
            pieces.append((T["d116h"][et, p0:p0 + 32], T["d1th"][et, p0:p0 + 32]))
    for eh in range(8):
        for p0 in range(0, 128, 16):
            pieces.append((T["d216h"][eh, p0:p0 + 16], T["d2th"][eh, p0:p0 + 16]))
    for et in range(4):
        for p0 in (0, 64):
            pieces.append((T["bc16h"][et, p0:p0 + 64], T["bcwth"][et, p0:p0 + 64]))
    add(T["gw16h"], T["gwh"], 128)
    return pieces


def phP(w, st):
    T = w.T
    pieces = _precast_pieces(T)
    pi = [0]
    pA, xs3 = T["pA"], T["xs3"]
    wch, wvs, pstg = T["wch"], T["wvs"], T["pstg"]

    ev_of, stv, tmm_of = {}, {}, {}
    gtile = [0]
    gchunk = [0]

    def pop_piece():
        if pi[0] < len(pieces):
            dst, src = pieces[pi[0]]
            dma(w, "gpsimd", dst, src, "qG")
            pi[0] += 1

    for h in range(HN):
        for kind in ("q", "k", "v"):
            wsrc = T[f"w{kind}T{h}h"]
            if kind in ("q", "k"):
                dst = T[(f"qtt{h}" if kind == "q" else f"kt{h}") + "h"]
                nchunks, tiles_per = KC, NT
            else:
                dst = T[f"v{h}h"]
                nchunks, tiles_per = 4, MC
            for dc in range(nchunks):
                c = gchunk[0]
                if c >= 2:
                    w.wait("gpsimd", "tMM", tmm_of[c - 2])
                if kind in ("q", "k"):
                    ld = dma(w, "gpsimd", wch[:, c % 2],
                             wsrc[:, dc * 128:(dc + 1) * 128]
                             .rearrange("(c p) m -> p c m", p=128), "qG")
                else:
                    ld = dma(w, "gpsimd", wvs[:, c % 2],
                             wsrc[:, dc * 512:(dc + 1) * 512]
                             .rearrange("(c p) m -> p c m", p=128), "qG")
                pop_piece()
                pop_piece()
                w.waitt("tensor", ld)
                if c == 0:
                    w.waitt("tensor", st["xs"])
                for tn in range(tiles_per):
                    ti = gtile[0]
                    pt = pA[:, (ti % 3) * 512:(ti % 3) * 512 + 512]
                    if ti >= 3:
                        w.wait("tensor", "sEV", ev_of[ti - 3])
                    for k in range(KC):
                        if kind in ("q", "k"):
                            l, r = wch[:, c % 2, k, :], xs3[:, k, tn * 512:(tn + 1) * 512]
                        else:
                            l, r = xs3[:, k, tn * 128:(tn + 1) * 128], wvs[:, c % 2, k, :]
                        mm(w, pt, l, r, start=(k == 0), stop=(k == KC - 1),
                           inc=(("tMM", 1) if k == KC - 1 else None))
                    w.wait("scalar", "tMM", w.c["tMM"])
                    if ti >= 2:
                        w.waitt("scalar", stv[ti - 2])
                    ev_of[ti] = op(w, "scalar", "copy", pstg[:, ti % 2, :], pt,
                                   inc=("sEV", 1))
                    w.wait("sync", "sEV", ev_of[ti])
                    if kind == "q":
                        o = dst[tn, :, dc, :]
                    elif kind == "k":
                        o = dst[dc * 128:(dc + 1) * 128, tn * 512:(tn + 1) * 512]
                    else:
                        o = dst[tn * 128:(tn + 1) * 128, dc * 512:(dc + 1) * 512]
                    stv[ti] = dma(w, "sync", o, pstg[:, ti % 2, :], "qS")
                    gtile[0] += 1
                tmm_of[c] = w.c["tMM"]
                gchunk[0] += 1
    while pi[0] < len(pieces):
        pop_piece()
    st["P_mm"] = w.c["tMM"]
    st["P_qS"] = w.qmark("qS")
    st["P_qG"] = w.qmark("qG")
    st["P_sEV"] = w.c["sEV"]


def phA1(w, st, h):
    T = w.T
    pA = T["pA"]
    qts, sst, wstg, ktch = T["qts2"], T["sst"], T["wstg"], T["ktch"]
    zsl, rzsl, negsc = T["zsl"], T["rzsl"], T["negsc2"]
    qth, kth, wth = T[f"qtt{h}h"], T[f"kt{h}h"], T["wth"]

    prev_mm = st["P_mm"] if h == 0 else st[f"A2_mm_{h-1}"]
    prev_sev = st["P_sEV"] if h == 0 else st[f"A2_sEV_{h-1}"]
    prev_qS = st["P_qS"] if h == 0 else st[f"A2_qS_{h-1}"]
    w.waitq("sync", prev_qS)
    w.waitq("gpsimd", prev_qS)
    w.wait("gpsimd", "tMM", prev_mm)
    qv = []
    for nt in range(NT):
        if nt == 0:
            w.wait("sync", "tMM", prev_mm)
        qv.append(dma(w, "sync", qts[:, nt], qth[nt], "qS"))

    ktv, expv, stv, ev, mmv = {}, {}, {}, {}, {}
    for mc in range(MC):
        if mc >= 2:
            w.wait("gpsimd", "tMM", mmv[mc - 2])
        ktv[mc] = dma(w, "gpsimd", ktch[:, mc % 2],
                      kth[:, mc * 128:(mc + 1) * 128]
                      .rearrange("(c p) m -> p c m", p=128), "qG")
        w.waitt("tensor", ktv[mc])
        for nt in range(NT):
            if mc == 0:
                w.waitt("tensor", qv[nt])
                if nt == 0:
                    w.wait("tensor", "sEV", prev_sev)
            ti = mc * NT + nt
            pt = pA[:, (ti % 3) * 512:(ti % 3) * 512 + 512]
            if ti >= 3:
                w.wait("tensor", "sEV", ev[ti - 3])
            for k in range(KC):
                mm(w, pt, ktch[:, mc % 2, k, :], qts[:, nt, k, :],
                   start=(k == 0), stop=(k == KC - 1),
                   inc=(("tMM", 1) if k == KC - 1 else None))
            w.wait("scalar", "tMM", w.c["tMM"])
            ev[ti] = op(w, "scalar", "activation", sst[:, mc % 2, nt * 512:(nt + 1) * 512],
                        pt, AF.Copy, scale=SCALE, inc=("sEV", 1))
        mmv[mc] = w.c["tMM"]
        if mc >= 2:
            w.wait("vector", "sEV", expv[mc - 2])
        w.wait("vector", "sEV", ev[mc * NT + NT - 1])
        op(w, "vector", "reduce_max", negsc[:, mc % 2:mc % 2 + 1], sst[:, mc % 2, :],
           axis=AX.X, negate=True)
        nv = vcarry(w, ("vEV", 1))
        w.wait("scalar", "vEV", nv)
        if mc >= 2:
            w.waitt("scalar", stv[mc - 2])
        op(w, "scalar", "activation", wstg[:, mc % 2, :], sst[:, mc % 2, :],
           AF.Exp, bias=negsc[:, mc % 2:mc % 2 + 1], scale=1.0,
           accum_out=zsl[:, mc:mc + 1])
        expv[mc] = scarry(w, ("sEV", 1))
        w.wait("vector", "sEV", expv[mc])
        op(w, "vector", "reciprocal", rzsl[:, mc:mc + 1], zsl[:, mc:mc + 1],
           inc=("vEV", 1))
        w.wait("sync", "sEV", expv[mc])
        stv[mc] = dma(w, "sync", wth[:, :, mc, :].rearrange("t p n -> p t n"),
                      wstg[:, mc % 2, :], "qS")
    st[f"A1_mm_{h}"] = w.c["tMM"]
    st[f"A1_rz_{h}"] = w.c["vEV"]
    st[f"A1_sEV_{h}"] = w.c["sEV"]
    st[f"A1_qS_{h}"] = w.qmark("qS")


def phA2(w, st, h):
    T = w.T
    pA, pB = T["pA"], T["pB"]
    vs, wts, atstg = T["vs"], T["wts"], T["atstg"]
    rzsl = T["rzsl"]
    vh, wth, ath = T[f"v{h}h"], T["wth"], T["at_allh"]

    if h == 0:
        w.waitq("scalar", st["P_qS"])
    else:
        w.waitq("scalar", st[f"A2_qG_{h-1}"])

    vv = {}
    for pc in range(4):
        if pc == 0:
            w.wait("sync", "tMM", st[f"A1_mm_{h}"])
        vv[pc] = dma(w, "sync", vs[:, pc * 8:(pc + 1) * 8, :],
                     vh[pc * 1024:(pc + 1) * 1024, :]
                     .rearrange("(c p) d -> p c d", p=128), "qS")
    scv = {}
    for mc in range(MC):
        w.waitt("vector", vv[mc // 8])
        if mc == 0:
            w.wait("vector", "vEV", st[f"A1_rz_{h}"])
        scv[mc] = op(w, "vector", "tensor_scalar_mul", vs[:, mc, :], vs[:, mc, :],
                     rzsl[:, mc:mc + 1], inc=("vEV", 1))

    order = [7] + list(range(7))
    wtv, grp_ev, stv = {}, {}, {}
    gidx = 0
    w.waitq("sync", st[f"A1_qS_{h}"])
    for i, nt in enumerate(order):
        if i >= 2:
            w.wait("sync", "tMM", st[f"A2t_{h}_{i-2}"])
        wtv[i] = dma(w, "sync", wts[:, i % 2], wth[nt], "qS")
        w.waitt("tensor", wtv[i])
        for grp in range(4):
            pX = pA if grp % 2 == 0 else pB
            if gidx >= 2:
                w.wait("tensor", "sEV", grp_ev[gidx - 2])
            elif gidx == 0:
                w.wait("tensor", "sEV", st[f"A1_sEV_{h}"])
            for mc in range(MC):
                if i == 0 and grp == 0:
                    w.wait("tensor", "vEV", scv[mc])
                elif mc == 0:
                    w.wait("tensor", "vEV", scv[MC - 1])
                for dj in range(4):
                    d = grp * 4 + dj
                    mm(w, pX[:, dj * 512:(dj + 1) * 512],
                       vs[:, mc, d * 128:(d + 1) * 128], wts[:, i % 2, mc, :],
                       start=(mc == 0), stop=(mc == MC - 1),
                       inc=(("tMM", 1) if (mc == MC - 1 and dj == 3) else None))
            w.wait("scalar", "tMM", w.c["tMM"])
            if gidx >= 2:
                w.waitt("scalar", stv[gidx - 2])
            val = None
            for dj in range(4):
                val = op(w, "scalar", "activation",
                         atstg[:, gidx % 2, dj, :], pX[:, dj * 512:(dj + 1) * 512],
                         AF.Copy, scale=(1.0 / 3.0),
                         inc=(("sEV", 1) if dj == 3 else None))
            grp_ev[gidx] = val
            w.wait("gpsimd", "sEV", val)
            if h > 0:
                w.waitt("gpsimd", st[f"A2_st_{h-1}"][gidx])
            stv[gidx] = dma(w, "gpsimd",
                            ath[grp * 512:(grp + 1) * 512, nt * 512:(nt + 1) * 512]
                            .rearrange("(c p) n -> p c n", p=128),
                            atstg[:, gidx % 2],
                            "qG", accum=(None if h == 0 else ALU.add))
            gidx += 1
        st[f"A2t_{h}_{i}"] = w.c["tMM"]
    st[f"A2_mm_{h}"] = w.c["tMM"]
    st[f"A2_st_{h}"] = dict(stv)
    st[f"A2_qG_{h}"] = w.qmark("qG")
    st[f"A2_sEV_{h}"] = w.c["sEV"]
    st[f"A2_qS_{h}"] = w.qmark("qS")


def _ln_plain(w, u, outt, eps, sq):
    T = w.T
    negsum, negmean, ssq, veps, rv, rstd = (T["s1"], T["negmean"], T["ssq"], T["veps"],
                                            T["rv"], T["rstd"])
    w.wait("vector", "vEV", w.c["vEV"])
    w.wait("vector", "sEV", w.c["sEV"])
    op(w, "vector", "reduce_sum", negsum, u, axis=AX.X, negate=True)
    a = vcarry(w, ("vEV", 1))
    w.wait("scalar", "vEV", a)
    op(w, "scalar", "mul", negmean, negsum, 1.0 / 2048.0)
    b0 = scarry(w, ("sEV", 1))
    w.wait("scalar", "sEV", b0)
    op(w, "scalar", "activation", sq, u, AF.Square, bias=negmean, accum_out=ssq)
    c = scarry(w, ("sEV", 1))
    w.wait("vector", "sEV", c)
    tv = op(w, "vector", "tensor_scalar", veps, ssq, 1.0 / 2048.0, eps, ALU.mult,
            ALU.add, inc=("vEV", 1))
    w.wait("vector", "vEV", tv)
    vrsqrt(w, rstd, veps, T["nwh"], T["nwa"], T["nwc"])
    return op(w, "vector", "tensor_scalar", outt, u, negmean, rstd, ALU.add, ALU.mult,
              inc=("vEV", 1))


def phFC(w, st):
    T = w.T
    pA, pB = T["pA"], T["pB"]
    fc16s, atb, xt2 = T["fc16s"], T["atb"], T["xt2"]
    u2b, x1b = T["u2b"], T["x1b"]
    fcbr, x1tstg, x1til = T["fcbr"], T["x1tstg"], T["x1til"]

    w.wait("sync", "tMM", st["A2_mm_2"])
    w.waitq("sync", st["A2_qG_2"])
    w.waitq("sync", st["P_qG"])
    w.waitq("scalar", st["A2_qG_2"])
    w.wait("scalar", "tMM", st["A2_mm_2"])
    w.wait("gpsimd", "tMM", st["A2_mm_2"])
    fc16hv = T["fc16h"].rearrange("(c p) n -> p c n", p=128)
    fcv = [dma(w, "scalar" if j % 2 == 0 else "gpsimd",
               fc16s[:, 2 * j:2 * j + 2, :], fc16hv[:, 2 * j:2 * j + 2, :],
               "qA" if j % 2 == 0 else "qG") for j in range(8)]
    fbv = dma(w, "sync", fcbr, T["fcbrh"], "qS")

    st_x1, atv, xv, fcmm = {}, {}, {}, {}
    fcbank, xcons, lnvd, x1bv, tpv, tpcp, xstv = {}, {}, {}, {}, {}, {}, {}

    def stageA(i):
        if i >= 2:
            w.wait("sync", "tMM", fcmm[i - 2])
        atv[i] = dma(w, "sync", atb[:, i % 2],
                     T["at_allh"][:, i * 128:(i + 1) * 128]
                     .rearrange("(c p) n -> p c n", p=128), "qS")
        if i >= 2:
            w.wait("sync", "vEV", xcons[i - 2])
        xv[i] = dma(w, "sync", xt2[:, i % 2], T["xnh"][i * 128:(i + 1) * 128, :], "qS")
        w.waitt("tensor", atv[i])
        if i == 0:
            w.wait("tensor", "sEV", st["A2_sEV_2"])
        u = u2b[:, i % 2]
        if i >= 1:
            w.wait("tensor", "vEV", fcbank[i - 1])
        for k in range(KC):
            if i == 0 and k % 2 == 0:
                w.waitt("tensor", fcv[k // 2])
            for et in range(4):
                pt = pA[:, et * 512:(et + 1) * 512]
                mm(w, pt, atb[:, i % 2, k, :], fc16s[:, k, et * 512:(et + 1) * 512],
                   start=(k == 0), stop=(k == KC - 1),
                   inc=(("tMM", 1) if k == KC - 1 else None))
        fcmm[i] = w.c["tMM"]
        w.wait("vector", "tMM", fcmm[i])
        w.waitt("vector", xv[i])
        if i == 0:
            w.waitt("vector", fbv)
        if i >= 2:
            w.wait("vector", "vEV", lnvd[i - 2])
        fcbank[i] = op(w, "vector", "tensor_add", u, pA[:, 0:2048], fcbr,
                       inc=("vEV", 1))
        w.wait("vector", "vEV", fcbank[i])
        xcons[i] = op(w, "vector", "tensor_add", u, u, xt2[:, i % 2],
                      inc=("vEV", 1))

    def stageB(i):
        u = u2b[:, i % 2]
        if i >= 1:
            w.waitt("vector", st_x1[i - 1])
        lnvd[i] = _ln_plain(w, u, x1til, EPS_M, T["sqt"])
        w.wait("sync", "vEV", lnvd[i])
        st_x1[i] = dma(w, "sync", T["x1h"][i * 128:(i + 1) * 128, :], x1til, "qS")
        if i >= 1:
            w.wait("scalar", "tMM", tpv[i - 1])
        w.wait("scalar", "vEV", lnvd[i])
        x1bv[i] = op(w, "scalar", "copy", x1b, x1til, inc=("sEV", 1))
        w.wait("tensor", "sEV", x1bv[i])
        if i == 0:
            w.waitt("tensor", st["ident"])
        if i >= 2:
            w.wait("tensor", "sEV", tpcp[i - 2])
        pb16 = pB[:, (i % 2) * 1024:(i % 2) * 1024 + 1024].bitcast(BF16)
        for c in range(KC):
            op(w, "tensor", "transpose", pb16[:, c * 128:(c + 1) * 128],
               x1b[:, c * 128:(c + 1) * 128], T["ident16"],
               inc=(("tMM", 1) if c == KC - 1 else None))
        tpv[i] = w.c["tMM"]
        w.wait("scalar", "tMM", tpv[i])
        if i >= 2:
            w.waitt("scalar", xstv[i - 2])
        tpcp[i] = op(w, "scalar", "copy", x1tstg[:, i % 2], pb16, inc=("sEV", 1))
        w.wait("scalar", "sEV", tpcp[i])
        xstv[i] = dma(w, "scalar", T["x1Th"][i], x1tstg[:, i % 2], "qA")

    for i in range(MC):
        stageA(i)
        if i >= 1:
            stageB(i - 1)
    stageB(MC - 1)
    st["FC_mm"] = w.c["tMM"]
    st["FC_x1"] = w.qmark("qS")
    st["FC_x1T"] = w.qmark("qA")
    st["FC_vEV"] = w.c["vEV"]


def _row_ln_affine(w, src, dst, gam, bet, eps, tmp):
    T = w.T
    negsum, negmean, ssq, veps, rv, rstd = (T["ts1"], T["tnegmean"], T["tssq"],
                                            T["tveps"], T["trv"], T["trstd"])
    w.wait("vector", "vEV", w.c["vEV"])
    w.wait("vector", "sEV", w.c["sEV"])
    op(w, "vector", "reduce_sum", negsum, src, axis=AX.X, negate=True)
    a = vcarry(w, ("vEV", 1))
    w.wait("scalar", "vEV", a)
    op(w, "scalar", "mul", negmean, negsum, 1.0 / 2048.0)
    b0 = scarry(w, ("sEV", 1))
    w.wait("scalar", "sEV", b0)
    op(w, "scalar", "activation", tmp, src, AF.Square, bias=negmean, accum_out=ssq)
    c = scarry(w, ("sEV", 1))
    w.wait("vector", "sEV", c)
    tv = op(w, "vector", "tensor_scalar", veps, ssq, 1.0 / 2048.0, eps, ALU.mult,
            ALU.add, inc=("vEV", 1))
    w.wait("vector", "vEV", tv)
    vrsqrt(w, rstd, veps, T["tnwh"], T["tnwa"], T["tnwc"])
    t1 = op(w, "vector", "tensor_scalar", dst, src, negmean, rstd, ALU.add, ALU.mult,
            inc=("vEV", 1))
    w.wait("vector", "vEV", t1)
    t2 = op(w, "vector", "tensor_mul", dst, dst, gam, inc=("vEV", 1))
    w.wait("vector", "vEV", t2)
    return op(w, "vector", "tensor_add", dst, dst, bet, inc=("vEV", 1))


def _row_to_col(w, row16, ncols, pcol, dst16):
    T = w.T
    w.wait("tensor", "sEV", w.c["sEV"])
    for c in range(ncols):
        mm(w, pcol[:, c:c + 1], row16[0:1, c * 128:(c + 1) * 128],
           T["basis16"][0:1, 0:1], start=True, stop=True,
           inc=(("tMM", 1) if c == ncols - 1 else None))
    w.wait("scalar", "tMM", w.c["tMM"])
    return op(w, "scalar", "copy", dst16, pcol[:, 0:ncols], inc=("sEV", 1))


def phTH(w, st):
    T = w.T
    pA, pB = T["pA"], T["pB"]
    d1sl, d2sl = T["d1sl"], T["d2sl"]
    trow, tmpr, tvec, tn = T["trow"], T["tmpr"], T["tvec"], T["tn"]
    ing, inb, png, pnb = T["ing_s"], T["inb_s"], T["png_s"], T["pnb_s"]
    tn16, h1row16 = T["tn16"], T["h1row16"]
    tT16, h1T16 = T["tT16"], T["h1T16"]
    bcr = T["bcr"]

    w.waitq("sync", st["FC_x1"])
    w.waitq("sync", st["FC_x1T"])
    w.waitq("sync", st["P_qG"])
    w.wait("sync", "tMM", st["FC_mm"])
    w.wait("scalar", "tMM", st["FC_mm"])
    dma(w, "sync", trow, T["x1h"][N - 1:N, :], "qS")
    dma(w, "sync", ing, T["ingh"], "qS")
    dma(w, "sync", inb, T["inbh"], "qS")
    dma(w, "sync", png, T["pngh"], "qS")
    lv = dma(w, "sync", pnb, T["pnbh"], "qS")

    w.waitq("vector", w.qmark("qS"))
    w.wait("vector", "tMM", st["FC_mm"])
    t0v = _row_ln_affine(w, trow, tmpr, ing, inb, EPS_F, T["tmprow2"])
    w.wait("vector", "vEV", w.c["vEV"])
    cv = op(w, "vector", "tensor_copy", tn16, tmpr, inc=("vEV", 1))
    w.wait("tensor", "vEV", cv)
    w.wait("tensor", "gEV", st["consts"])
    rcv = _row_to_col(w, tn16, KC, pB[:, 0:16], tT16)
    w.wait("tensor", "sEV", rcv)

    def slab17(buf, srcmat, et, eng="sync", q="qS"):
        t1 = dma(w, eng, buf[:, :, :].rearrange("p c n -> p (c n)"),
                 srcmat[et], q)
        return [t1]

    def mm17(pt, slab, colT, nk):
        for k in range(nk):
            mm(w, pt, colT[:, k:k + 1], slab[:, k, :], start=(k == 0), stop=False)
        mm(w, pt, T["basis16"][:, 0:1], slab[:, nk, :], start=False, stop=True,
           inc=("tMM", 1))
        return w.c["tMM"]

    d1cnt = [0]
    d1mm = {}

    def d1buf():
        c = d1cnt[0]
        d1cnt[0] += 1
        return c, c % 3

    aggv, pev = {}, {}
    for et in range(4):
        c, b = d1buf()
        eng, q = (("sync", "qS") if et % 2 == 0 else ("scalar", "qA"))
        if c >= 3:
            w.wait(eng, "tMM", d1mm[c - 3])
        v = slab17(d1sl[:, b], T["agg16h"], et, eng, q)
        w.waitq("tensor", v)
        if et >= 3:
            w.wait("tensor", "vEV", pev[et - 3])
        pt = pA[0:1, (et % 3) * 512:(et % 3) * 512 + 512]
        aggv[et] = mm17(pt, d1sl[:, b], tT16, KC)
        d1mm[c] = aggv[et]
        w.wait("vector", "tMM", aggv[et])
        pev[et] = op(w, "vector", "tensor_copy", tvec[0:1, et * 512:(et + 1) * 512], pt,
                     inc=("vEV", 1))

    h1g = {}
    d2mm = {}
    for s in range(4):
        _row_ln_affine(w, tvec, tn, png, pnb, EPS_F, T["tmprow2"])
        w.wait("vector", "vEV", w.c["vEV"])
        cv = op(w, "vector", "tensor_copy", tn16, tn, inc=("vEV", 1))
        w.wait("tensor", "vEV", cv)
        rcv = _row_to_col(w, tn16, KC, pB[:, 0:16], tT16)
        w.wait("tensor", "sEV", rcv)
        for ht in range(16):
            c, b = d1buf()
            eng, q = (("sync", "qS") if ht % 2 == 0 else ("scalar", "qA"))
            if c >= 3:
                w.wait(eng, "tMM", d1mm[c - 3])
            v = slab17(d1sl[:, b], T["d116h"], ht, eng, q)
            w.waitq("tensor", v)
            pt = pA[0:1, (ht % 3) * 512:(ht % 3) * 512 + 512]
            if ht >= 3:
                w.wait("tensor", "sEV", h1g[(s, ht - 3, "g")])
            h1g[(s, ht, "mm")] = mm17(pt, d1sl[:, b], tT16, KC)
            d1mm[c] = h1g[(s, ht, "mm")]
            w.wait("scalar", "tMM", h1g[(s, ht, "mm")])
            h1g[(s, ht, "g")] = op(w, "scalar", "activation",
                                   h1row16[0:1, ht * 512:(ht + 1) * 512], pt, AF.Gelu,
                                   inc=("sEV", 1))
        w.wait("tensor", "sEV", h1g[(s, 15, "g")])
        h1c = _row_to_col(w, h1row16, 64, pB[:, 16:80], h1T16)
        w.wait("tensor", "sEV", h1c)
        for eh in range(8):
            eng, q = (("sync", "qS") if eh % 2 == 0 else ("scalar", "qA"))
            if eh >= 2:
                w.wait(eng, "tMM", d2mm[(s, eh - 2)])
            elif s > 0:
                w.wait(eng, "tMM", d2mm[(s - 1, 6 + eh)])
            elif eh == 0:
                w.wait(eng, "vEV", t0v)
            v1 = dma(w, eng, d2sl[:, eh % 2].rearrange("p c n -> p (c n)"),
                     T["d216h"][eh], q)
            w.waitt("tensor", v1)
            pt = pA[0:1, (eh % 3) * 512:(eh % 3) * 512 + 256]
            if eh >= 3:
                w.wait("tensor", "vEV", d2mm[(s, eh - 3, "v")])
            for k in range(64):
                mm(w, pt, h1T16[:, k:k + 1], d2sl[:, eh % 2, k, :],
                   start=(k == 0), stop=False)
            mm(w, pt, T["basis16"][:, 0:1], d2sl[:, eh % 2, 64, :],
               start=False, stop=True, inc=("tMM", 1))
            d2mm[(s, eh)] = w.c["tMM"]
            w.wait("vector", "tMM", d2mm[(s, eh)])
            d2mm[(s, eh, "v")] = op(w, "vector", "tensor_add",
                                    tvec[0:1, eh * 256:(eh + 1) * 256],
                                    tvec[0:1, eh * 256:(eh + 1) * 256], pt,
                                    inc=("vEV", 1))

    w.wait("vector", "vEV", w.c["vEV"])
    cv = op(w, "vector", "tensor_copy", tn16, tvec, inc=("vEV", 1))
    w.wait("tensor", "vEV", cv)
    rcv = _row_to_col(w, tn16, KC, pB[:, 0:16], tT16)
    w.wait("tensor", "sEV", rcv)
    bcv = {}
    for et in range(4):
        c, b = d1buf()
        eng, q = (("sync", "qS") if et % 2 == 0 else ("scalar", "qA"))
        if c >= 3:
            w.wait(eng, "tMM", d1mm[c - 3])
        v = slab17(d1sl[:, b], T["bc16h"], et, eng, q)
        w.waitq("tensor", v)
        pt = pA[0:1, (et % 3) * 512:(et % 3) * 512 + 512]
        if et >= 3:
            w.wait("tensor", "vEV", bcv[(et - 3, "v")])
        bcv[et] = mm17(pt, d1sl[:, b], tT16, KC)
        d1mm[c] = bcv[et]
        w.wait("vector", "tMM", bcv[et])
        bcv[(et, "v")] = op(w, "vector", "tensor_copy",
                            bcr[0:1, et * 512:(et + 1) * 512], pt, inc=("vEV", 1))
    w.wait("tensor", "vEV", bcv[(3, "v")])
    for c in range(4):
        mm(w, pA[:, c * 512:(c + 1) * 512], T["onesf"][0:1, :],
           bcr[0:1, c * 512:(c + 1) * 512], start=True, stop=True,
           inc=(("tMM", 1) if c == 3 else None))
    w.wait("vector", "tMM", w.c["tMM"])
    st["bct_v"] = op(w, "vector", "tensor_copy", T["bct"], pA[:, 0:2048], inc=("vEV", 1))
    st["TH_mm"] = w.c["tMM"]


def phGATE(w, st):
    T = w.T
    pA = T["pA"]
    gw16s, x1tch, x1t2, gt2 = T["gw16s"], T["x1tch"], T["x1t2"], T["gt2"]
    ot, ft, u2t = T["ot"], T["ft"], T["u2t"]
    outstg = T["outstg"]
    gbr, ogr, obr, bct = T["gbr"], T["ogr"], T["obr"], T["bct"]

    w.wait("sync", "tMM", st["TH_mm"])
    w.wait("sync", "vEV", st["bct_v"])
    w.waitq("sync", st["FC_x1"])
    w.waitq("sync", st["FC_x1T"])
    w.wait("scalar", "tMM", st["TH_mm"])
    w.wait("gpsimd", "tMM", st["TH_mm"])
    gw16hv = T["gw16h"].rearrange("(c p) n -> p c n", p=128)
    gv = [dma(w, "scalar" if j % 2 == 0 else "gpsimd",
              gw16s[:, 2 * j:2 * j + 2, :], gw16hv[:, 2 * j:2 * j + 2, :],
              "qA" if j % 2 == 0 else "qG") for j in range(8)]
    st["gbr_v"] = dma(w, "sync", T["gbr"], T["gbrh"], "qS")
    st["ogr_v"] = dma(w, "sync", T["ogr"], T["ogrh"], "qS")
    st["obr_v"] = dma(w, "sync", T["obr"], T["obrh"], "qS")

    xtv, x1v, mmv, stv, u2v, sgv, gaddv, gconsv = {}, {}, {}, {}, {}, {}, {}, {}
    pB = T["pB"]

    def stageA(i):
        gt = gt2[:, i % 2]
        pX = pA if i % 2 == 0 else pB
        if i >= 2:
            w.wait("sync", "tMM", mmv[i - 2])
        xtv[i] = dma(w, "sync", x1tch[:, i % 2].rearrange("p c n -> p (c n)"),
                     T["x1Th"][i], "qS")
        if i >= 2:
            w.wait("sync", "vEV", u2v[i - 2])
        x1v[i] = dma(w, "sync", x1t2[:, i % 2], T["x1h"][i * 128:(i + 1) * 128, :], "qS")
        w.waitt("tensor", xtv[i])
        if i == 0:
            w.wait("tensor", "vEV", st["bct_v"])
        if i >= 2:
            w.wait("tensor", "vEV", gaddv[i - 2])
        for k in range(KC):
            if i == 0 and k % 2 == 0:
                w.waitt("tensor", gv[k // 2])
            for et in range(4):
                pt = pX[:, et * 512:(et + 1) * 512]
                mm(w, pt, x1tch[:, i % 2, k, :], gw16s[:, k, et * 512:(et + 1) * 512],
                   start=(k == 0), stop=(k == KC - 1),
                   inc=(("tMM", 1) if k == KC - 1 else None))
        mmv[i] = w.c["tMM"]
        w.wait("vector", "tMM", mmv[i])
        if i == 0:
            w.waitt("vector", st["gbr_v"])
        if i >= 2:
            w.wait("vector", "vEV", gconsv[i - 2])
        gaddv[i] = op(w, "vector", "tensor_add", gt, pX[:, 0:2048], gbr, inc=("vEV", 1))
        w.wait("scalar", "vEV", gaddv[i])
        op(w, "scalar", "activation", gt, gt, AF.Sigmoid, inc=("sEV", 1))
        sgv[i] = w.c["sEV"]

    def stageB(i):
        gt = gt2[:, i % 2]
        w.wait("vector", "sEV", sgv[i])
        if i == 0:
            w.waitt("vector", st["ogr_v"])
            w.waitt("vector", st["obr_v"])
        gconsv[i] = op(w, "vector", "tensor_mul", ot, gt, bct, inc=("vEV", 1))
        w.wait("vector", "vEV", gconsv[i])
        w.waitt("vector", x1v[i])
        op(w, "vector", "tensor_add", ot, ot, x1t2[:, i % 2], inc=("vEV", 1))
        lff = _ln_plain(w, ot, ft, EPS_F, T["sqg"])
        w.wait("vector", "vEV", lff)
        t1 = op(w, "vector", "tensor_mul", ft, ft, ogr, inc=("vEV", 1))
        w.wait("vector", "vEV", t1)
        t2 = op(w, "vector", "tensor_add", ft, ft, obr, inc=("vEV", 1))
        w.wait("vector", "vEV", t2)
        u2v[i] = op(w, "vector", "tensor_add", u2t, ft, x1t2[:, i % 2], inc=("vEV", 1))
        if i >= 2:
            w.waitt("vector", stv[i - 2])
        lnv = _ln_plain(w, u2t, outstg[:, i % 2], EPS_M, T["sqg"])
        w.wait("sync", "vEV", lnv)
        stv[i] = dma(w, "sync", T["outh"][i * 128:(i + 1) * 128, :], outstg[:, i % 2], "qS")

    for i in range(MC):
        stageA(i)
        if i >= 1:
            stageB(i - 1)
    stageB(MC - 1)



def build_nc(dbg=False):
    nc = bass.Bass("TRN2", target_bir_lowering=False)
    T = {}

    def din(name, shape, dt=F32):
        T[name + "h"] = nc.dram_tensor(name, shape, dt, kind="ExternalInput").ap()

    def dscr(name, shape, dt):
        kind = "ExternalOutput" if dbg else "Internal"
        T[name + "h"] = nc.dram_tensor(name, shape, dt, kind=kind).ap()

    din("xT", [E, N]); din("xn", [N, E])
    for h in range(HN):
        din(f"wqT{h}", [E, E]); din(f"wkT{h}", [E, E]); din(f"wvT{h}", [E, E])
    din("fcT", [E, E]); din("gw", [E, E])
    din("aggt", [4, 128, 17, 512]); din("d1t", [16, 128, 17, 512])
    din("d2t", [8, 128, 65, 256]); din("bcwt", [4, 128, 17, 512])
    din("fcbr", [128, E]); din("gbr", [128, E]); din("ogr", [128, E]); din("obr", [128, E])
    din("ing", [1, E]); din("inb", [1, E]); din("png", [1, E]); din("pnb", [1, E])
    din("aggb", [1, E]); din("d2b", [1, E]); din("bcb", [1, E]); din("d1b", [1, HID])
    din("ident", [128, 128], BF16)
    T["outh"] = nc.dram_tensor("out", [N, E], F32, kind="ExternalOutput").ap()

    for h in range(HN):
        dscr(f"qt{h}", [E, N], BF16); dscr(f"kt{h}", [E, N], BF16)
        dscr(f"v{h}", [N, E], BF16)
    dscr("at_all", [E, N], BF16)
    dscr("wt", [NT, 128, MC, 512], BF16)
    dscr("x1", [N, E], F32); dscr("x1T", [MC, 128, E], BF16)
    for h in range(HN):
        dscr(f"qtt{h}", [NT, 128, KC, 512], BF16)
    dscr("fc16", [E, E], BF16); dscr("agg16", [4, 128, 17, 512], BF16)
    dscr("d116", [16, 128, 17, 512], BF16); dscr("d216", [8, 128, 65, 256], BF16)
    dscr("bc16", [4, 128, 17, 512], BF16); dscr("gw16", [E, E], BF16)

    A = nc.alloc_sbuf_tensor("arenaA", [128, 131072], mybir.dt.uint8).ap()
    B = nc.alloc_sbuf_tensor("arenaB", [128, 65536], mybir.dt.uint8).ap()
    R = nc.alloc_sbuf_tensor("rowsA", [128, 8192], mybir.dt.uint8).ap()
    S = nc.alloc_sbuf_tensor("statsA", [128, 2048], mybir.dt.uint8).ap()
    A16, A32 = A.bitcast(BF16), A.bitcast(F32)
    B16, B32 = B.bitcast(BF16), B.bitcast(F32)
    R16, R32 = R.bitcast(BF16), R.bitcast(F32)
    S16, S32 = S.bitcast(BF16), S.bitcast(F32)

    T["xs3"] = A16[:, :].rearrange("p (c n) -> p c n", c=KC)
    T["wch"] = B16[:, 0:2 * KC * 128].rearrange("p (b c m) -> p b c m", b=2, c=KC)
    T["wvs"] = B16[:, 4096:4096 + 2 * KC * 512].rearrange("p (b c m) -> p b c m", b=2, c=KC)
    T["pstg"] = R16[:, 0:1024].rearrange("p (b n) -> p b n", b=2)
    T["qts2"] = A16[:, :].rearrange("p (t c n) -> p t c n", t=NT, c=KC)
    T["sst"] = B32[:, 0:8192].rearrange("p (b n) -> p b n", b=2)
    T["wstg"] = B16[:, 16384:24576].rearrange("p (b n) -> p b n", b=2)
    T["ktch"] = B16[:, 24576:28672].rearrange("p (b c m) -> p b c m", b=2, c=KC)
    T["vs"] = A16[:, :].rearrange("p (c d) -> p c d", c=MC)
    T["wts"] = B16[:, :].rearrange("p (b c n) -> p b c n", b=2, c=MC)
    T["atstg"] = R16[:, 0:4096].rearrange("p (b c n) -> p b c n", b=2, c=4)
    T["fc16s"] = A16[:, 0:KC * 2048].rearrange("p (c n) -> p c n", c=KC)
    T["atb"] = A16[:, 32768:32768 + 2 * 2048].rearrange("p (b c n) -> p b c n",
                                                        b=2, c=KC)
    T["xt2"] = A32[:, 23552:23552 + 4096].rearrange("p (b n) -> p b n", b=2)
    T["u2b"] = B32[:, 6144:10240].rearrange("p (b n) -> p b n", b=2)
    T["sqt"] = B32[:, 10240:12288]
    T["x1b"] = A16[:, 63488:63488 + 2048]
    T["fcbr"] = B32[:, 0:2048]
    T["x1til"] = B32[:, 2048:4096]
    T["x1tstg"] = B16[:, 8192:12288].rearrange("p (b n) -> p b n", b=2)
    T["d1sl"] = A16[:, 0:3 * 17 * 512].rearrange("p (b c n) -> p b c n", b=3, c=17)
    T["d2sl"] = A16[:, 26112:26112 + 2 * 65 * 256].rearrange("p (b c n) -> p b c n",
                                                             b=2, c=65)
    T["ing_s"] = A32[0:1, 13056:15104]
    T["inb_s"] = A32[0:1, 15104:17152]
    T["tmpr"] = A32[0:1, 29696:31744]
    T["tmprow2"] = T["tmpr"]
    T["trow"] = R32[0:1, 0:2048]
    T["png_s"] = B32[0:1, 2048:4096]
    T["pnb_s"] = B32[0:1, 4096:6144]
    T["tvec"] = B32[0:1, 6144:8192]
    T["tn"] = B32[0:1, 8192:10240]
    T["bcr"] = B32[0:1, 8192:10240]
    T["h1row16"] = B16[0:1, 20480:28672]
    T["tn16"] = B16[0:1, 28672:30720]
    T["gw16s"] = A16[:, 0:KC * 2048].rearrange("p (c n) -> p c n", c=KC)
    T["x1tch"] = A16[:, 32768:36864].rearrange("p (b c n) -> p b c n", b=2, c=KC)
    T["x1t2"] = A32[:, 18432:22528].rearrange("p (b n) -> p b n", b=2)
    T["gt2"] = B32[:, 12288:16384].rearrange("p (b n) -> p b n", b=2)
    T["ot"] = A32[:, 24576:26624]
    T["ft"] = A32[:, 26624:28672]
    T["sqg"] = A32[:, 28672:30720]
    T["u2t"] = A32[:, 30720:32768]
    T["bct"] = B32[0:128, 0:2048]
    T["gbr"] = B32[:, 2048:4096]
    T["ogr"] = B32[:, 4096:6144]
    T["obr"] = B32[:, 6144:8192]
    T["outstg"] = B32[:, 8192:12288].rearrange("p (b n) -> p b n", b=2)
    T["s1"] = S32[:, 0:1]; T["negmean"] = S32[:, 1:2]; T["ssq"] = S32[:, 3:4]
    T["veps"] = S32[:, 4:5]; T["rv"] = S32[:, 5:6]; T["rstd"] = S32[:, 6:7]
    T["cmax"] = S32[:, 7:8]
    T["ts1"] = S32[0:1, 8:9]; T["tnegmean"] = S32[0:1, 9:10]; T["tssq"] = S32[0:1, 10:11]
    T["tveps"] = S32[0:1, 11:12]; T["trv"] = S32[0:1, 12:13]; T["trstd"] = S32[0:1, 13:14]
    T["zsl"] = S32[:, 128:160]; T["rzsl"] = S32[:, 160:192]
    T["cm8"] = S32[:, 192:200]; T["negsc2"] = S32[:, 200:202]
    T["tT16"] = S16[:, 416:432]; T["h1T16"] = S16[:, 432:496]
    T["ident16"] = S16[:, 496:624]
    T["basis16"] = T["ident16"][:, 0:1]
    T["onesf"] = S32[0:1, 336:464]

    T["negsl"] = R32[:, 1024:1040]
    T["flushd"] = S32[:, 14:15]
    T["flushs"] = S32[:, 15:16]
    T["flushd2"] = S32[:, 202:203]
    T["nwh"] = S32[:, 16:17]; T["nwa"] = S32[:, 17:18]; T["nwc"] = S32[:, 18:19]
    T["tnwh"] = S32[0:1, 20:21]; T["tnwa"] = S32[0:1, 21:22]; T["tnwc"] = S32[0:1, 22:23]
    T["pA"] = nc.alloc_psum_tensor("pA", [128, 2048], F32).ap()
    T["pB"] = nc.alloc_psum_tensor("pB", [128, 2048], F32).ap()

    from contextlib import ExitStack
    es = ExitStack()
    sems = {s: es.enter_context(nc.semaphore(s)) for s in SEMS}

    with nc.Block() as blk:
        def mk(curname):
            def f(eng):
                emit_all(W(curname, eng, sems, T))
            return f
        blk.tensor(mk("tensor"))
        blk.vector(mk("vector"))
        blk.scalar(mk("scalar"))
        blk.sync(mk("sync"))
        blk.gpsimd(mk("gpsimd"))
    return nc



_NC_CACHE = {}


def _get_nc(dbg=False):
    if dbg not in _NC_CACHE:
        _NC_CACHE[dbg] = build_nc(dbg)
    return _NC_CACHE[dbg]


def make_in_maps(L0_attn_x, L0_attn_Wq, L0_attn_Wk, L0_attn_Wv, L0_attn_fc_w,
                 L0_attn_fc_b, L0_ffn_in_gamma, L0_ffn_in_beta, L0_ffn_agg_w,
                 L0_ffn_agg_b, L0_ffn_pn_gamma, L0_ffn_pn_beta, L0_ffn_d1_w,
                 L0_ffn_d1_b, L0_ffn_d2_w, L0_ffn_d2_b, L0_ffn_bc_w, L0_ffn_bc_b,
                 L0_ffn_gate_w, L0_ffn_gate_b, L0_ffn_out_gamma, L0_ffn_out_beta):
    f = np.float32
    x = np.asarray(L0_attn_x, f)
    shared = {}
    for h in range(HN):
        shared[f"wqT{h}"] = np.ascontiguousarray(np.asarray(L0_attn_Wq, f)[h].T)
        shared[f"wkT{h}"] = np.ascontiguousarray(np.asarray(L0_attn_Wk, f)[h].T)
        shared[f"wvT{h}"] = np.ascontiguousarray(np.asarray(L0_attn_Wv, f)[h].T)
    shared["fcT"] = np.ascontiguousarray(np.asarray(L0_attn_fc_w, f).T)
    shared["gw"] = np.ascontiguousarray(np.asarray(L0_ffn_gate_w, f))

    def tile_aug(wmat, bias, tcount, csz, nsz):
        out = np.zeros((tcount, 128, csz + 1, nsz), f)
        out[:, :, 0:csz, :] = wmat.reshape(csz, 128, tcount, nsz).transpose(2, 1, 0, 3)
        out[:, 0, csz, :] = bias.reshape(tcount, nsz)
        return out

    shared["aggt"] = tile_aug(np.asarray(L0_ffn_agg_w, f), np.asarray(L0_ffn_agg_b, f),
                              4, 16, 512)
    shared["d1t"] = tile_aug(np.asarray(L0_ffn_d1_w, f), np.asarray(L0_ffn_d1_b, f),
                             16, 16, 512)
    shared["d2t"] = tile_aug(np.asarray(L0_ffn_d2_w, f), np.asarray(L0_ffn_d2_b, f),
                             8, 64, 256)
    shared["bcwt"] = tile_aug(np.asarray(L0_ffn_bc_w, f), np.asarray(L0_ffn_bc_b, f),
                              4, 16, 512)
    shared["fcbr"] = np.ascontiguousarray(np.broadcast_to(np.asarray(L0_attn_fc_b, f), (128, E)))
    shared["gbr"] = np.ascontiguousarray(np.broadcast_to(np.asarray(L0_ffn_gate_b, f), (128, E)))
    shared["ogr"] = np.ascontiguousarray(np.broadcast_to(np.asarray(L0_ffn_out_gamma, f), (128, E)))
    shared["obr"] = np.ascontiguousarray(np.broadcast_to(np.asarray(L0_ffn_out_beta, f), (128, E)))
    shared["ing"] = np.asarray(L0_ffn_in_gamma, f).reshape(1, E)
    shared["inb"] = np.asarray(L0_ffn_in_beta, f).reshape(1, E)
    shared["png"] = np.asarray(L0_ffn_pn_gamma, f).reshape(1, E)
    shared["pnb"] = np.asarray(L0_ffn_pn_beta, f).reshape(1, E)
    shared["aggb"] = np.asarray(L0_ffn_agg_b, f).reshape(1, E)
    shared["d2b"] = np.asarray(L0_ffn_d2_b, f).reshape(1, E)
    shared["bcb"] = np.asarray(L0_ffn_bc_b, f).reshape(1, E)
    shared["d1b"] = np.asarray(L0_ffn_d1_b, f).reshape(1, HID)
    shared["ident"] = np.eye(128, dtype=ml_dtypes.bfloat16)
    in_maps = []
    for c in range(8):
        m = dict(shared)
        m["xT"] = np.ascontiguousarray(x[c].T)
        m["xn"] = np.ascontiguousarray(x[c])
        in_maps.append(m)
    return in_maps


def kernel(**inputs):
    in_maps = make_in_maps(**inputs)
    nc = _get_nc(dbg=False)
    res = run_bass_kernel_spmd(nc, in_maps, core_ids=list(range(8)))
    return np.stack([r["out"] for r in res.results], axis=0)
